# version 09
# baseline (speedup 1.0000x reference)
import os
import numpy as np
import ml_dtypes
import concourse.bass as bass
import concourse.mybir as mybir
from concourse.bass_utils import run_bass_kernel_spmd
from contextlib import ExitStack

F32 = mybir.dt.float32
BF16 = mybir.dt.bfloat16
F16 = mybir.dt.float16
F8 = mybir.dt.float8e4
I32 = mybir.dt.int32
AF = mybir.ActivationFunctionType
OP = mybir.AluOpType

D, DI, NL, NST, RR = 1024, 2048, 4, 16, 64
LC, TC = 2048, 512
NCH = LC // TC
ND = D // 128
NJ = DI // 128
XCW = TC + 3

OFF_CONVW, OFF_CONVB, OFF_DP, OFF_DTB = 0, 256, 320, 384
OFF_NORMW, OFF_FINW, OFF_C = 448, 480, 488
OFF_B1, OFF_B2, OFF_SEL = 496, 528, 536
NCOLS = 544

DC_NEGDTB, DC_SILUC, DC_MOD = 0, 64, 72
DC_ONEPCA, DC_ONEPCB, DC_GA, DC_GB = 120, 128, 136, 144
DC_B1Q, DC_B2Q = 152, 184
DCW = 192


class Step:
    __slots__ = ("idx", "eng", "fn", "deps", "inc", "sem_val", "is_async",
                 "reg_wait", "tag", "sem")

    def __init__(self, idx, eng, fn, deps, inc, is_async, tag):
        self.idx, self.eng, self.fn = idx, eng, fn
        self.deps, self.inc, self.is_async = deps, inc, is_async
        self.reg_wait = None
        self.tag = tag
        self.sem = None


class Prog:
    ENGS = ("sync", "vector", "scalar", "tensor", "gpsimd")
    NSY, NGP = 6, 4

    def __init__(self):
        self.steps = []
        self.writer = {}
        self.readers = {}
        self._syc = 0
        self._gpc = 0

    def step(self, eng, fn, reads=(), writes=(), inc=1, deps=(),
             is_async=False, no_raw=False, tag=""):
        idx = len(self.steps)
        dep = set()
        for d in deps:
            if d is not None:
                dep.add(d if isinstance(d, int) else d.idx)
        if not no_raw:
            for b in reads:
                if b in self.writer:
                    dep.add(self.writer[b])
        for b in writes:
            if b in self.writer:
                dep.add(self.writer[b])
            for r in self.readers.get(b, ()):
                dep.add(r)
        st = Step(idx, eng, fn, dep, inc, is_async, tag)
        if eng == "sync":
            st.sem = f"sync{self._syc % self.NSY}"
            self._syc += 1
        elif eng == "gpsimd" and inc % 16 == 0 and inc > 0:
            st.sem = f"gpd{self._gpc % self.NGP}"
            self._gpc += 1
        else:
            st.sem = eng
        self.steps.append(st)
        for b in reads:
            self.readers.setdefault(b, []).append(idx)
        for b in writes:
            self.writer[b] = idx
            self.readers[b] = []
        return st

    def finalize(self):
        cnt = {}
        for st in self.steps:
            cnt[st.sem] = cnt.get(st.sem, 0) + st.inc
            st.sem_val = cnt[st.sem]
        self.totals = cnt


def build(n_layers=NL, do_ffn=True, dbg=None, groups=None):
    nc = bass.Bass("TRN2", target_bir_lowering=False, debug=False,
                   detect_race_conditions=False)

    def din(name, shape, dt=F32):
        return nc.dram_tensor(name, list(shape), dt, kind="ExternalInput").ap()

    def dint(name, shape, dt):
        return nc.dram_tensor(name, list(shape), dt, kind="Internal").ap()

    xin = din("xin", [D, LC])
    maskin = din("maskin", [D, LC], F16)
    cols_d = din("cols", [128, NCOLS])
    alog_d = din("alog", [128, NL * 256])
    id8_d = din("id8", [128, 128], F8)
    flags_d = din("flags", [1, 2], I32)
    w_in_d = din("w_in", [NL * 8, 128, 4096])
    w_out_d = din("w_out", [NL * 16, 128, 1024])
    w_xp_d = din("w_xp", [NL, 128, 16 * 96])
    w_dt_d = din("w_dt", [NL, 64, 2048])
    w1_d = din("w1", [8, 128, 4096])
    w2_d = din("w2", [32, 128, 1024])
    wmod_d = din("wmod", [8, 128, 6144])
    wcd_d = din("wconvd", [NL * 16, 128, 512], BF16)

    xout = nc.dram_tensor("xout", [D, LC], F32, kind="ExternalOutput").ap()
    dbg_out = None
    if dbg is not None:
        dbg_out = nc.dram_tensor("dbg", [128, dbg[1]], dbg[2],
                                 kind="ExternalOutput").ap()

    resid = dint("resid", [D, LC], F32)
    wSin = dint("wSin", [NL * 8, 128, 4096], BF16)
    wSout = dint("wSout", [NL * 16, 128, 1024], BF16)
    wSxp = dint("wSxp", [NL, 128, 16 * 96], BF16)
    wSdt = dint("wSdt", [NL, 64, 2048], BF16)
    wS1 = dint("wS1", [8, 128, 4096], F8)
    wS2 = dint("wS2", [32, 128, 1024], F8)
    modscr = dint("modscr", [1, 6144], F32)
    bcscr = dint("bcscr", [32, TC], BF16)
    cc_in = dint("cc_in", [128, 304], F32)
    cc_out = dint("cc_out", [2, 128, 304], F32)

    es = ExitStack()

    def sbt(name, shape, dt=F32):
        return es.enter_context(nc.sbuf_tensor(name, shape, dt))

    cols = sbt("cols_s", [128, NCOLS])
    apos = sbt("apos", [128, NL * 256])
    dcol = sbt("dcol", [128, DCW])
    sgn = sbt("sgn", [128, 16])
    id8_s = sbt("id8_s", [128, 128], F8)
    bq_s = sbt("bq_s", [128, 40], F8)
    flags = sbt("flags_s", [1, 2], I32)
    ones_c = sbt("ones_c", [128, 1])
    ones_cb = sbt("ones_cb", [128, 1], BF16)
    ones_rf = sbt("ones_rf", [1, 128])
    epsln = sbt("epsln", [1, 1])
    epsrms = sbt("epsrms", [1, 1])
    eps37 = sbt("eps37", [128, 1])
    statm = sbt("statm", [1, TC])
    stati = sbt("stati", [1, TC])
    xcur = sbt("xcur", [128, ND * TC])
    xcr = sbt("xcr", [128, NJ * XCW], BF16)
    xcs = sbt("xcs", [128, NJ * TC], BF16)
    szb = sbt("szb", [128, NJ * TC], BF16)
    dlt = sbt("dlt", [128, NJ * TC], BF16)
    u_s = sbt("u_s", [128, NJ * TC], BF16)
    brep = sbt("brep", [128, NST * TC], BF16)
    crep = sbt("crep", [128, NST * TC], BF16)
    hpool = sbt("hpool", [128, NST * TC], BF16)
    hcpool = sbt("hcpool", [128, NST * TC], BF16)
    ytot = sbt("ytot", [128, NJ * TC], BF16)
    abar = [sbt("abar0", [128, TC], BF16), sbt("abar1", [128, TC], BF16)]
    statv = ytot[:].bitcast(F32)[0:1, 0:TC]
    bxb = sbt("bxb", [128, TC], BF16)
    xpb = sbt("xpb", [96, TC], BF16)
    state = sbt("state", [128, 256])
    tailbuf = sbt("tailbuf", [128, 48], BF16)
    ccres = sbt("ccres", [128, 304])
    wi = [sbt(f"wi{k}", [128, 512], BF16) for k in range(8)]
    wo = [sbt("wo0", [128, 1024], BF16), sbt("wo1", [128, 1024], BF16)]
    wxpt = [sbt("wxpt0", [128, 96], BF16), sbt("wxpt1", [128, 96], BF16)]
    wdtt = [sbt("wdtt0", [64, 128], BF16), sbt("wdtt1", [64, 128], BF16)]

    xhat = hpool[:, 0:4096]
    xorig = crep[:].bitcast(F32)
    xsq = xcr[:].bitcast(F32)[:, 0:4096]
    xf32 = xcr[:].bitcast(F32)[:, 0:4096]
    h8v = u_s[:].bitcast(F8)
    xqv = szb[:].bitcast(F8)
    hflo = hpool[:].bitcast(F16)
    hfhi = hcpool[:].bitcast(F16)
    y2v = dlt[:].bitcast(F16)
    maskv = brep[:].bitcast(F16)
    wf8a = xcs[:].bitcast(F8)
    wf8b = xcr[:].bitcast(F8)
    wo8 = [w[:].bitcast(F8) for w in wo]
    prep_in = [hpool[:].bitcast(F32), hcpool[:].bitcast(F32)]
    prep_out = [ytot[:, 0:4096], ytot[:, 4096:8192]]
    prep_out8 = ytot[:].bitcast(F8)


    ps = {n: es.enter_context(nc.psum_tensor(n, [128, TC], F32))
          for n in ("psA", "psB", "psC", "psD", "psE", "psF", "psG", "psH")}
    PSN = list(ps)

    semnames = list(Prog.ENGS) + [f"sync{i}" for i in range(Prog.NSY)] \
        + [f"gpd{i}" for i in range(Prog.NGP)]
    sems = {e: es.enter_context(nc.semaphore("sem_" + e)) for e in semnames}

    P = Prog()
    CURSEM = [None]

    def ccol(off):
        return cols[:, off:off + 1]

    def dco(off):
        return dcol[:, off:off + 1]

    def sl(buf, j, w=TC):
        return buf[:, j * w:(j + 1) * w]

    P.step("sync", lambda e: e.dma_start(out=cols[:], in_=cols_d)
           .then_inc(CURSEM[0], 16), writes=["cols"], inc=16,
           is_async=True, tag="ld_cols")
    P.step("sync", lambda e: e.dma_start(out=flags[:], in_=flags_d)
           .then_inc(CURSEM[0], 16), writes=["flags"], inc=16,
           is_async=True, tag="ld_flags")

    def init_consts(e):
        e.memset(ones_c[:], 1.0)
        e.memset(ones_cb[:], 1.0)
        e.memset(ones_rf[:], 1.0)
        e.memset(epsln[:], 1e-5)
        e.memset(epsrms[:], 1e-6)
        e.memset(eps37[:], 1e-37)
        e.memset(state[:], 0.0)
        e.memset(ccres[:], 0.0)
        e.memset(xcr[:], 0.0)
        return e.engine_nop()
    st_ccinit = P.step("vector", init_consts,
                       writes=["state", "consts", "xcr"], tag="init")

    P.step("sync", lambda e: e.dma_start(
        out=xcur[:].bitcast(F32)[:, 1024:2048], in_=alog_d)
        .then_inc(CURSEM[0], 16), writes=["xcur"], inc=16, is_async=True,
        tag="ld_alog")
    P.step("scalar", lambda e: e.activation(
        out=apos[:], in_=xcur[:].bitcast(F32)[:, 1024:2048], func=AF.Exp),
        reads=["xcur"], writes=["apos"], tag="apos")
    P.step("vector", lambda e: e.tensor_scalar_mul(
        out=dcol[:, DC_NEGDTB:DC_NEGDTB + 64],
        in0=cols[:, OFF_DTB:OFF_DTB + 64], scalar1=-1.0),
        reads=["cols"], writes=["dcol"], tag="negdtb")
    P.step("sync", lambda e: e.dma_start(out=id8_s[:], in_=id8_d)
           .then_inc(CURSEM[0], 16), writes=["id8"], inc=16, is_async=True,
           tag="ld_id8")
    P.step("vector", lambda e: e.tensor_copy(
        out=bq_s[:], in_=cols[:, OFF_B1:OFF_B1 + 40]),
        reads=["cols"], writes=["bq_s"], tag="bq8")
    P.step("tensor", lambda e: e.matmul(
        ps["psE"][:, 0:40], id8_s[:], bq_s[:], start=True, stop=True),
        reads=["bq_s", "id8"], writes=["psE"], tag="bqmm")
    P.step("scalar", lambda e: e.copy(
        out=dcol[:, DC_B1Q:DC_B1Q + 40], in_=ps["psE"][:, 0:40]),
        reads=["psE"], writes=["dcol"], tag="bqf")
    P.step("scalar", lambda e: e.activation(
        out=dcol[:, DC_SILUC:DC_SILUC + 8], in_=cols[:, OFF_C:OFF_C + 8],
        func=AF.Silu), reads=["cols"], writes=["dcol"], tag="siluc")

    prep_specs = []
    for l in range(NL):
        for k in range(8):
            prep_specs.append((("win", l, k), w_in_d[l * 8 + k],
                               wSin[l * 8 + k], 128, 4096, BF16))
        prep_specs.append((("wxp", l), w_xp_d[l], wSxp[l], 128, 16 * 96, BF16))
        prep_specs.append((("wdt", l), w_dt_d[l], wSdt[l], 64, 2048, BF16))
        for k in range(16):
            prep_specs.append((("wout", l, k), w_out_d[l * 16 + k],
                               wSout[l * 16 + k], 128, 1024, BF16))
    for k in range(8):
        prep_specs.append((("w1", k), w1_d[k], wS1[k], 128, 4096, F8))
    for k in range(32):
        prep_specs.append((("w2", k), w2_d[k], wS2[k], 128, 1024, F8))

    prep_rdy = {}
    PINF = hpool[:].bitcast(F32)[:, 2560:3072]
    POUTB = hpool[:, 6144:6656]
    POUT8 = hpool[:].bitcast(F8)[:, 13312:13824]
    last_prep = [None]

    def emit_prep_spec(i, pkey, src_, dst, prows, width, odt):
        for off in range(0, width, 512):
            w = min(512, width - off)
            pin = PINF[0:prows, 0:w]
            pout = (POUTB if odt == BF16 else POUT8)[0:prows, 0:w]

            def f_ld(e, src_=src_, pin=pin, prows=prows, off=off, w=w):
                return e.dma_start(out=pin, in_=src_[0:prows, off:off + w]) \
                    .then_inc(CURSEM[0], 16)

            def f_cast(e, pin=pin, pout=pout):
                return e.tensor_copy(out=pout, in_=pin)

            def f_acast(e, pin=pin, pout=pout):
                return e.activation(out=pout, in_=pin, func=AF.Identity)

            def f_st(e, dst=dst, pout=pout, prows=prows, off=off, w=w):
                return e.dma_start(out=dst[0:prows, off:off + w], in_=pout) \
                    .then_inc(CURSEM[0], 16)

            P.step("sync", f_ld, writes=["prepin"], inc=16, is_async=True,
                   tag=f"prep_ld{i}_{off}")
            if odt == BF16:
                P.step("scalar", f_acast, reads=["prepin"],
                       writes=["prepout"], tag=f"prep_cast{i}_{off}")
            else:
                P.step("vector", f_cast, reads=["prepin"],
                       writes=["prepout"], tag=f"prep_cast{i}_{off}")
            st_prep = P.step("sync", f_st, reads=["prepout"],
                             writes=[f"wS{i}"], inc=16, is_async=True,
                             tag=f"prep_st{i}_{off}")
        prep_rdy[pkey] = st_prep
        last_prep[0] = st_prep

    prep_by_layer = {0: [], 1: [], 2: [], 3: [], "ffn": []}
    for i, spec in enumerate(prep_specs):
        pkey = spec[0]
        if pkey[0] in ("w1", "w2"):
            prep_by_layer["ffn"].append((i, spec))
        else:
            prep_by_layer[pkey[1]].append((i, spec))
    for i, spec in prep_by_layer[0]:
        emit_prep_spec(i, *spec)

    prep_pending = []

    def prep_emit_some(k):
        for _ in range(min(k, len(prep_pending))):
            i, spec = prep_pending.pop(0)
            emit_prep_spec(i, *spec)

    def prep_drain():
        prep_emit_some(len(prep_pending))

    def prep_refill(l, ch):
        if l < NL - 1:
            grp = prep_by_layer[l + 1]
            n = (len(grp) + 2) // 3
            if ch < 3:
                prep_pending.extend(grp[ch * n:(ch + 1) * n])
            else:
                prep_drain()
        else:
            grp = prep_by_layer["ffn"]
            n = (len(grp) + 3) // 4
            prep_pending.extend(grp[ch * n:(ch + 1) * n])

    prep_hp_guard = [P.writer.get("hpool")] + list(P.readers.get("hpool", ()))

    def win_rdy(l, k):
        return prep_rdy[("win", l, k)]
    def wout_rdy(l, k):
        return prep_rdy[("wout", l, k)]
    def wxp_rdy(l):
        return prep_rdy[("wxp", l)]
    def wdt_rdy(l):
        return prep_rdy[("wdt", l)]
    def w1_rdy(k):
        return prep_rdy[("w1", k)]
    def w2_rdy(k):
        return prep_rdy[("w2", k)]

    wmbf32 = xcr[:].bitcast(F32)
    for grp in range(2):
        for s in range(8):
            def f_wm(e, s=s, grp=grp):
                return e.dma_start(
                    out=wmbf32[:, 0:3072],
                    in_=wmod_d[s, :, grp * 3072:(grp + 1) * 3072]) \
                    .then_inc(CURSEM[0], 16)
            P.step("sync", f_wm, reads=[], writes=["xcr"], inc=16,
                   is_async=True, tag=f"modw{grp}{s}")
            for p in range(6):
                def f_mm(e, s=s, p=p):
                    return e.matmul(
                        ps[PSN[p]][0:1, :],
                        dcol[:, DC_SILUC + s:DC_SILUC + s + 1],
                        wmbf32[:, p * TC:(p + 1) * TC],
                        start=(s == 0), stop=(s == 7))
                P.step("tensor", f_mm, reads=["xcr", "dcol"],
                       writes=[f"{PSN[p]}"], tag=f"modmm{grp}{s}{p}")
        for p in range(6):
            def f_ev(e, p=p):
                return e.copy(out=statm[:], in_=ps[PSN[p]][0:1, :])
            P.step("scalar", f_ev, reads=[f"{PSN[p]}"], writes=["statm"],
                   tag=f"modev{grp}{p}")
            pi = grp * 6 + p
            def f_st(e, pi=pi):
                return e.dma_start(
                    out=modscr[0:1, pi * TC:(pi + 1) * TC], in_=statm[:]) \
                    .then_inc(CURSEM[0], 16)
            P.step("gpsimd", f_st, reads=["statm"], writes=["modscr"],
                   inc=16, is_async=True, tag=f"modst{grp}{p}")
    P.step("sync", lambda e: e.dma_start(
        out=dcol[:, DC_MOD:DC_MOD + 48],
        in_=modscr[0].rearrange("(b r p) -> p (b r)", b=6, r=8, p=128),
        allow_slow_non_contiguous=True)
        .then_inc(CURSEM[0], 16), reads=["modscr"], writes=["dcol"],
        inc=16, is_async=True, tag="modld")

    def drv0(e):
        e.activation(out=dcol[:, DC_GA:DC_GA + 8],
                     in_=dcol[:, DC_MOD + 16:DC_MOD + 24], func=AF.Abs)
        return e.activation(out=dcol[:, DC_GB:DC_GB + 8],
                            in_=dcol[:, DC_MOD + 40:DC_MOD + 48], func=AF.Abs)
    P.step("scalar", drv0, reads=["dcol"], writes=["dcol"], tag="drv0")

    def drv(e):
        e.tensor_scalar_add(out=dcol[:, DC_ONEPCA:DC_ONEPCA + 8],
                            in0=dcol[:, DC_MOD + 8:DC_MOD + 16], scalar1=1.0)
        e.tensor_scalar_add(out=dcol[:, DC_ONEPCB:DC_ONEPCB + 8],
                            in0=dcol[:, DC_MOD + 32:DC_MOD + 40], scalar1=1.0)
        e.tensor_scalar_max(out=dcol[:, DC_GA:DC_GA + 8],
                            in0=dcol[:, DC_GA:DC_GA + 8], scalar1=0.25)
        return e.tensor_scalar_max(out=dcol[:, DC_GB:DC_GB + 8],
                                   in0=dcol[:, DC_GB:DC_GB + 8], scalar1=0.25)
    P.step("vector", drv, reads=["dcol"], writes=["dcol"], tag="drv")

    def sgnf(e):
        e.activation(out=sgn[:, 0:8], in_=dcol[:, DC_MOD + 16:DC_MOD + 24],
                     func=AF.Sign, bias=eps37[:])
        return e.activation(out=sgn[:, 8:16],
                            in_=dcol[:, DC_MOD + 40:DC_MOD + 48],
                            func=AF.Sign, bias=eps37[:])
    P.step("scalar", sgnf, reads=["dcol", "consts"], writes=["sgn"], tag="sgn")

    def gmul(e):
        e.tensor_mul(out=dcol[:, DC_GA:DC_GA + 8],
                     in0=dcol[:, DC_GA:DC_GA + 8], in1=sgn[:, 0:8])
        return e.tensor_mul(out=dcol[:, DC_GB:DC_GB + 8],
                            in0=dcol[:, DC_GB:DC_GB + 8], in1=sgn[:, 8:16])
    P.step("vector", gmul, reads=["sgn", "dcol"], writes=["dcol"], tag="gmul")

    XCRS = [f"xcr{j}" for j in range(NJ)]
    def f_hclr(e):
        e.memset(tailbuf[:], 0.0)
        return e.memset(xcr[:], 0.0)
    st_haloclr = P.step("vector", f_hclr,
        writes=["xcr", "tailbuf", "xcrhalo"] + XCRS, tag="haloclr")

    def stats_ssq(srcname, src_view, psn, tag, ones=None):
        rd = [srcname] if isinstance(srcname, str) else list(srcname)
        onesap = ones_c if ones is None else ones
        for r in range(ND):
            def f(e, r=r, onesap=onesap):
                return e.matmul(ps[psn][0:1, :], onesap[:],
                                sl(src_view, r), start=(r == 0),
                                stop=(r == ND - 1))
            P.step("tensor", f, reads=rd + ["consts"], writes=[psn],
                   tag=f"{tag}{r}")

    def rms_inv(psn, epsap, tag):
        P.step("scalar", lambda e: e.activation(
            out=stati[:], in_=ps[psn][0:1, :], func=AF.Sqrt,
            scale=1.0 / D, bias=epsap), reads=[psn, "consts"],
            writes=["stati"], tag=tag + "s")
        P.step("vector", lambda e: e.reciprocal(
            out=stati[:], in_=stati[:]), reads=["stati"],
            writes=["stati"], tag=tag + "r")

    def bcast(rowname, row_ap, psn, tag):
        P.step("tensor", lambda e: e.matmul(
            ps[psn][:], ones_rf[:], row_ap, start=True, stop=True),
            reads=[rowname, "consts"], writes=[psn], tag=tag)

    wiD = [hpool[:, 4096:4608], hpool[:, 4608:5120]]
    gate_patches = []
    ccdma_steps = {}
    init_steps = {}
    NG = NL * NCH

    for ch in range(NCH):
        cr = f"S{ch}"
        t0 = ch * TC
        def f_ldx(e, t0=t0):
            return e.dma_start(
                out=xorig[:, 0:ND * TC].rearrange("p (r t) -> p r t", t=TC),
                in_=xin.rearrange("(r p) t -> p r t", p=128)
                [:, :, t0:t0 + TC]).then_inc(CURSEM[0], 16)
        P.step("sync", f_ldx, writes=["crep"], inc=16, is_async=True,
               tag=f"{cr}_ldx")
        P.step("scalar", lambda e: e.activation(
            out=xsq[:, 0:ND * TC], in_=xorig[:, 0:ND * TC], func=AF.Square),
            reads=["crep"], writes=XCRS + ["xcrhalo"], tag=f"{cr}_sq")
        stats_ssq("crep", xorig, "psE", f"{cr}_sx")
        stats_ssq(XCRS, xsq, "psF", f"{cr}_sxx")
        def f_rows(e):
            e.tensor_scalar_mul(out=statm[:], in0=ps["psE"][0:1, :],
                                scalar1=1.0 / D)
            e.tensor_mul(out=stati[:], in0=statm[:], in1=statm[:])
            return e.scalar_tensor_tensor(
                out=stati[:], in0=ps["psF"][0:1, :], scalar=1.0 / D,
                in1=stati[:], op0=OP.mult, op1=OP.subtract)
        P.step("vector", f_rows, reads=["psE", "psF"],
               writes=["statm", "stati"], tag=f"{cr}_rows")
        P.step("scalar", lambda e: e.activation(
            out=stati[:], in_=stati[:], func=AF.Sqrt, bias=epsln[:]),
            reads=["stati", "consts"], writes=["stati"], tag=f"{cr}_invs")
        P.step("vector", lambda e: e.reciprocal(
            out=stati[:], in_=stati[:]), reads=["stati"],
            writes=["stati"], tag=f"{cr}_invr")
        bcast("statm", statm[:], "psG", f"{cr}_bm")
        bcast("stati", stati[:], "psH", f"{cr}_bi")
        for r in range(ND):
            def f_ap(e, r=r):
                e.tensor_sub(out=sl(xsq, r), in0=sl(xorig, r),
                             in1=ps["psG"][:])
                e.tensor_mul(out=sl(xsq, r), in0=sl(xsq, r),
                             in1=ps["psH"][:])
                return e.tensor_scalar(
                    out=sl(xcur, r), in0=sl(xsq, r),
                    scalar1=dco(DC_ONEPCA + r), scalar2=dco(DC_MOD + r),
                    op0=OP.mult, op1=OP.add)
            P.step("vector", f_ap,
                   reads=["crep", "psG", "psH", "dcol"],
                   writes=["xcur"] + XCRS, tag=f"{cr}_ap{r}")
        def f_st0(e, t0=t0):
            return e.dma_start(
                out=resid.rearrange("(r p) t -> p r t", p=128)
                [:, :, t0:t0 + TC],
                in_=xcur[:].rearrange("p (r t) -> p r t", t=TC)) \
                .then_inc(CURSEM[0], 16)
        P.step("sync", f_st0, reads=["xcur"], writes=["resid"],
               inc=16, is_async=True, tag=f"{cr}_st0")

    def emit_ldrf32(l, ch):
        t0 = ch * TC
        def f_ldr(e, t0=t0):
            return e.dma_start(
                out=xcur[:].rearrange("p (r t) -> p r t", t=TC),
                in_=resid.rearrange("(r p) t -> p r t", p=128)
                [:, :, t0:t0 + TC]).then_inc(CURSEM[0], 16)
        P.step("sync", f_ldr, reads=["resid"], writes=["xcur"],
               inc=16, is_async=True, tag=f"L{l}c{ch}_ldr")

    def emit_abgrp(l, ch, j):
        cr = f"L{l}c{ch}"
        for n in range(NST):
            ab = abar[n % 2]
            def f_ab(e, j=j, n=n, l=l, ab=ab):
                return e.activation(
                    out=ab[:], in_=sl(dlt, j), func=AF.Exp,
                    scale=apos[:, l * 256 + j * 16 + n:
                               l * 256 + j * 16 + n + 1])
            P.step("scalar", f_ab, reads=[f"dlt{j}", "apos"],
                   writes=[f"abar{n % 2}"], tag=f"{cr}_ab{j}_{n}")

    def emit_B(l, ch):
        cr = f"L{l}c{ch}"
        t0 = ch * TC
        def f_ldb(e, t0=t0):
            return e.dma_start(
                out=xf32.rearrange("p (r t) -> p r t", t=TC),
                in_=resid.rearrange("(r p) t -> p r t", p=128)
                [:, :, t0:t0 + TC]).then_inc(CURSEM[0], 16)
        P.step("sync", f_ldb, reads=["resid"], writes=XCRS + ["xcrhalo"],
               inc=16, is_async=True, tag=f"{cr}_ldb")
        P.step("scalar", lambda e: e.activation(
            out=xhat[:, 0:ND * TC], in_=xf32[:, 0:ND * TC], func=AF.Square),
            reads=XCRS, writes=["hpool"], tag=f"{cr}_rsq")
        stats_ssq("hpool", xhat, "psF", f"{cr}_rss", ones=ones_cb)
        rms_inv("psF", epsrms[:], f"{cr}_rinv")
        bcast("stati", stati[:], "psH", f"{cr}_rbi")
        for r in range(ND):
            def f_rap(e, r=r, l=l):
                return e.scalar_tensor_tensor(
                    out=sl(xhat, r), in0=sl(xf32, r),
                    scalar=ccol(OFF_NORMW + l * 8 + r), in1=ps["psH"][:],
                    op0=OP.mult, op1=OP.mult)
            P.step("vector", f_rap, reads=XCRS + ["psH", "cols"],
                   writes=["hpool"], tag=f"{cr}_rap{r}")

    def emit_C(l, ch):
        cr = f"L{l}c{ch}"
        for g in range(4):
            for k in range(8):
                def f_wld(e, l=l, k=k, g=g):
                    return e.dma_start(
                        out=wi[k][:], in_=wSin[l * 8 + k, :,
                                               g * 512:(g + 1) * 512]) \
                        .then_inc(CURSEM[0], 16)
                P.step("sync", f_wld, writes=[f"wi{k}"], inc=16,
                       is_async=True, deps=[win_rdy(l, k)],
                       tag=f"{cr}_wi{g}{k}")
            for mm in range(4):
                m = g * 4 + mm
                pA = PSN[m % 4]
                for k in range(8):
                    def f_mm(e, m=m, mm=mm, k=k, pA=pA):
                        return e.matmul(
                            ps[pA][:], wi[k][:, mm * 128:(mm + 1) * 128],
                            sl(xhat, k), start=(k == 0), stop=(k == 7))
                    P.step("tensor", f_mm,
                           reads=[f"wi{k}", "hpool"], writes=[pA],
                           tag=f"{cr}_ip{m}_{k}")
                def f_ev(e, m=m, pA=pA):
                    return e.copy(
                        out=xcr[:, m * XCW + 3:m * XCW + 3 + TC],
                        in_=ps[pA][:])
                P.step("scalar", f_ev, reads=[pA],
                       writes=[f"xcr{m}"], tag=f"{cr}_evx{m}")

    def emit_zgrp(l, ch, K):
        cr = f"L{l}c{ch}"
        g = 4 + K
        for k in range(8):
            def f_wld(e, l=l, k=k, g=g):
                return e.dma_start(
                    out=wi[k][:], in_=wSin[l * 8 + k, :,
                                           g * 512:(g + 1) * 512]) \
                    .then_inc(CURSEM[0], 16)
            P.step("sync", f_wld, writes=[f"wi{k}"], inc=16,
                   is_async=True, deps=[win_rdy(l, k)],
                   tag=f"{cr}_wi{g}{k}")
        for mm in range(4):
            m = g * 4 + mm
            pA = PSN[m % 4]
            for k in range(8):
                def f_mm(e, m=m, mm=mm, k=k, pA=pA):
                    return e.matmul(
                        ps[pA][:], wi[k][:, mm * 128:(mm + 1) * 128],
                        sl(xhat, k), start=(k == 0), stop=(k == 7))
                P.step("tensor", f_mm,
                       reads=[f"wi{k}", "hpool"], writes=[pA],
                       tag=f"{cr}_ip{m}_{k}")
            def f_ev(e, m=m, pA=pA):
                return e.activation(
                    out=sl(szb, m - NJ), in_=ps[pA][:],
                    func=AF.Silu)
            P.step("scalar", f_ev, reads=[pA],
                   writes=[f"szb{m - NJ}"], tag=f"{cr}_evz{m}")

    def emit_happ(l, ch):
        cr = f"L{l}c{ch}"
        def f_happ(e):
            return e.tensor_copy(
                out=xcr[:].rearrange("p (j w) -> p j w", w=XCW)[:, :, 0:3],
                in_=tailbuf[:].rearrange("p (j w) -> p j w", w=3))
        P.step("vector", f_happ, reads=["tailbuf"], writes=["xcrhalo"],
               tag=f"{cr}_happ")

    def emit_conv_slice(l, ch, j):
        cr = f"L{l}c{ch}"
        wt = wiD[j % 2]
        wtn = f"wiD{j % 2}"
        def f_wdl(e, l=l, j=j, wt=wt):
            return e.dma_start(out=wt, in_=wcd_d[l * 16 + j]) \
                .then_inc(CURSEM[0], 16)
        P.step("sync", f_wdl, writes=[wtn], inc=16, is_async=True,
               deps=prep_hp_guard, tag=f"{cr}_wdl{j}")
        pc = "psC" if j % 2 == 0 else "psD"
        for k in range(4):
            def f_cmm(e, j=j, k=k, pc=pc, wt=wt):
                base = j * XCW
                return e.matmul(ps[pc][:], wt[:, k * 128:(k + 1) * 128],
                                xcr[:, base + k:base + k + TC],
                                start=(k == 0), stop=(k == 3))
            P.step("tensor", f_cmm, reads=[wtn, f"xcr{j}", "xcrhalo"],
                   writes=[pc], tag=f"{cr}_cv{j}_{k}")
        def f_cs(e, j=j, l=l, pc=pc):
            return e.activation(
                out=sl(xcs, j), in_=ps[pc][:], func=AF.Silu,
                bias=ccol(OFF_CONVB + l * 16 + j))
        P.step("scalar", f_cs, reads=[pc, "cols"], writes=[f"xcs{j}"],
               tag=f"{cr}_cs{j}")
        def f_xpl(e, l=l, k=j):
            return e.dma_start(out=wxpt[k % 2][:],
                               in_=wSxp[l, :, k * 96:(k + 1) * 96]) \
                .then_inc(CURSEM[0], 16)
        P.step("sync", f_xpl, writes=[f"wxpt{j % 2}"], inc=16,
               is_async=True, deps=[wxp_rdy(l)], tag=f"{cr}_xpl{j}")
        def f_xp(e, k=j):
            return e.matmul(ps["psE"][0:96, :], wxpt[k % 2][:],
                            sl(xcs, k), start=(k == 0),
                            stop=(k == NJ - 1))
        P.step("tensor", f_xp, reads=[f"wxpt{j % 2}", f"xcs{j}"],
               writes=["psE"], tag=f"{cr}_xp{j}")

    def emit_xpe_bcst(l, ch):
        cr = f"L{l}c{ch}"
        P.step("scalar", lambda e: e.copy(out=xpb[:], in_=ps["psE"][0:96, :]),
               reads=["psE"], writes=["xpb"], tag=f"{cr}_xpe")
        P.step("sync", lambda e: e.dma_start(out=bcscr, in_=xpb[64:96, :])
               .then_inc(CURSEM[0], 16), reads=["xpb"],
               writes=["bcscr"], inc=16, is_async=True, tag=f"{cr}_bcst")

    def emit_bcs(l, ch):
        cr = f"L{l}c{ch}"
        def f_bc_s(e):
            for n in range(NST):
                e.dma_start(out=sl(brep, n),
                            in_=bcscr[n:n + 1, :].broadcast_to([128, TC])) \
                    .then_inc(CURSEM[0], 16)
            return None
        P.step("sync", f_bc_s, reads=["bcscr"], writes=["brep"],
               inc=16 * NST, is_async=True, tag=f"{cr}_bcs")

    def emit_bcg(l, ch):
        cr = f"L{l}c{ch}"
        def f_bc_g(e):
            for n in range(NST):
                e.dma_start(out=sl(crep, n),
                            in_=bcscr[NST + n:NST + n + 1, :]
                            .broadcast_to([128, TC])) \
                    .then_inc(CURSEM[0], 16)
            return None
        P.step("sync", f_bc_g, reads=["bcscr"], writes=["crep"],
               inc=16 * NST, is_async=True, tag=f"{cr}_bcg")

    def emit_dth(l, ch, half):
        cr = f"L{l}c{ch}"
        for q8 in range(8):
            mt = half * 8 + q8
            pA = PSN[q8]
            def f_dtl(e, l=l, mt=mt):
                return e.dma_start(
                    out=wdtt[mt % 2][:],
                    in_=wSdt[l, :, mt * 128:(mt + 1) * 128]) \
                    .then_inc(CURSEM[0], 16)
            P.step("sync", f_dtl, writes=[f"wdtt{mt % 2}"], inc=16,
                   is_async=True, deps=[wdt_rdy(l)],
                   tag=f"{cr}_dtl{mt}")
            def f_dt(e, mt=mt, pA=pA):
                return e.matmul(ps[pA][:], wdtt[mt % 2][:],
                                xpb[0:64, :], start=True, stop=True)
            P.step("tensor", f_dt, reads=[f"wdtt{mt % 2}", "xpb"],
                   writes=[pA], tag=f"{cr}_dt{mt}")
        for q8 in range(8):
            mt = half * 8 + q8
            pA = PSN[q8]
            def f_sg(e, mt=mt, l=l, pA=pA):
                return e.activation(
                    out=ps[pA][:], in_=ps[pA][:], func=AF.Sigmoid,
                    scale=-1.0, bias=dco(DC_NEGDTB + l * 16 + mt))
            P.step("scalar", f_sg, reads=[pA, "dcol"], writes=[pA],
                   tag=f"{cr}_sg{mt}")
        for q8 in range(8):
            mt = half * 8 + q8
            pA = PSN[q8]
            def f_ln(e, mt=mt, pA=pA):
                return e.activation(
                    out=sl(dlt, mt), in_=ps[pA][:], func=AF.Ln,
                    bias=eps37[:])
            P.step("scalar", f_ln, reads=[pA, "consts"],
                   writes=[f"dlt{mt}"], tag=f"{cr}_ln{mt}")

    def emit_Ftail(l, ch):
        cr = f"L{l}c{ch}"
        emit_xpe_bcst(l, ch)
        emit_bcs(l, ch)
        emit_dth(l, ch, 0)
        emit_dth(l, ch, 1)
        emit_bcg(l, ch)
        def f_tail(e):
            return e.tensor_copy(
                out=tailbuf[:].rearrange("p (j w) -> p j w", w=3),
                in_=xcr[:].rearrange("p (j w) -> p j w", w=XCW)
                [:, :, TC:TC + 3])
        P.step("vector", f_tail, reads=XCRS, writes=["tailbuf"],
               tag=f"{cr}_tail")
        for mt in range(NJ):
            def f_u(e, mt=mt):
                return e.tensor_mul(out=sl(u_s, mt), in0=sl(dlt, mt),
                                    in1=sl(xcs, mt))
            P.step("vector", f_u, reads=[f"dlt{mt}", f"xcs{mt}"],
                   writes=[f"u{mt}"], tag=f"{cr}_u{mt}")

    def emit_F0(l, ch):
        emit_happ(l, ch)
        for j in range(NJ):
            emit_conv_slice(l, ch, j)
        for K in range(4):
            emit_zgrp(l, ch, K)
        emit_Ftail(l, ch)

    def emit_Dj(l, ch, j):
        cr = f"L{l}c{ch}"
        def f_bxp(e, j=j):
            return e.tensor_mul(
                out=hcpool[:].rearrange("p (n t) -> p n t", t=TC),
                in0=sl(u_s, j).unsqueeze(1)
                .broadcast_to([128, NST, TC]),
                in1=brep[:].rearrange("p (n t) -> p n t", t=TC))
        P.step("vector", f_bxp, reads=[f"u{j}", "brep"],
               writes=["hcpool"], tag=f"{cr}_bxp{j}")
        for n in range(NST):
            ab = abar[n % 2]
            def f_sc(e, j=j, n=n, ab=ab):
                return e.tensor_tensor_scan(
                    out=sl(hcpool, n), data0=ab[:], data1=sl(hcpool, n),
                    initial=state[:, j * 16 + n:j * 16 + n + 1],
                    op0=OP.mult, op1=OP.add)
            P.step("vector", f_sc,
                   reads=[f"abar{n % 2}", "hcpool", "state"],
                   writes=["hcpool"], tag=f"{cr}_sc{j}_{n}")
        def f_st(e, j=j):
            return e.tensor_copy(
                out=state[:, j * 16:(j + 1) * 16],
                in_=hcpool[:].rearrange("p (n t) -> p n t", t=TC)
                [:, :, TC - 1])
        P.step("vector", f_st, reads=["hcpool"], writes=["state"],
               tag=f"{cr}_stc{j}")
        def f_hcp(e, j=j):
            return e.tensor_mul(out=hcpool[:], in0=hcpool[:],
                                in1=crep[:])
        P.step("vector", f_hcp, reads=["hcpool", "crep"],
               writes=["hcpool"], tag=f"{cr}_hcp{j}")
        def f_tr(e, j=j):
            for st_ in (1, 2, 4):
                v = hcpool[:].rearrange("p (a g t) -> p a g t",
                                        g=2 * st_, t=TC)
                e.tensor_add(out=v[:, :, 0, :],
                             in0=v[:, :, 0, :],
                             in1=v[:, :, st_, :])
            return e.tensor_add(out=sl(ytot, j), in0=sl(hcpool, 0),
                                in1=sl(hcpool, 8))
        P.step("vector", f_tr, reads=["hcpool"],
               writes=[f"ytot{j}", "hcpool"], tag=f"{cr}_tr{j}")
        def f_yd(e, j=j, l=l):
            return e.scalar_tensor_tensor(
                out=sl(ytot, j), in0=sl(xcs, j),
                scalar=ccol(OFF_DP + l * 16 + j), in1=sl(ytot, j),
                op0=OP.mult, op1=OP.subtract)
        P.step("vector", f_yd,
               reads=[f"xcs{j}", "cols", f"ytot{j}"],
               writes=[f"ytot{j}"], tag=f"{cr}_yd{j}")
        def f_yg(e, j=j):
            return e.tensor_mul(out=sl(ytot, j), in0=sl(ytot, j),
                                in1=sl(szb, j))
        st_yg = P.step("vector", f_yg,
                       reads=[f"ytot{j}", f"szb{j}"], writes=[f"ytot{j}"],
                       tag=f"{cr}_yg{j}")
        if j + 1 < NJ:
            emit_abgrp(l, ch, j + 1)
        return st_yg

    def emit_E_mm(l, ch):
        cr = f"L{l}c{ch}"
        for j in range(NJ):
            def f_wo(e, l=l, j=j):
                return e.dma_start(out=wo[j % 2][:],
                                   in_=wSout[l * 16 + j]) \
                    .then_inc(CURSEM[0], 16)
            P.step("sync", f_wo, writes=[f"wo{j % 2}"], inc=16,
                   is_async=True, deps=[wout_rdy(l, j)],
                   tag=f"{cr}_wo{j}")
            for m in range(8):
                def f_op(e, k=j, m=m):
                    return e.matmul(
                        ps[PSN[m]][:], wo[k % 2][:, m * 128:(m + 1) * 128],
                        sl(ytot, k), start=(k == 0), stop=(k == NJ - 1))
                P.step("tensor", f_op, reads=[f"wo{j % 2}", f"ytot{j}"],
                       writes=[PSN[m]], tag=f"{cr}_op{j}_{m}")

    def emit_E_fin(l, ch):
        cr = f"L{l}c{ch}"
        t0 = ch * TC
        for m in range(8):
            def f_ra(e, m=m):
                return e.tensor_add(out=sl(xcur, m), in0=ps[PSN[m]][:],
                                    in1=sl(xcur, m))
            P.step("vector", f_ra, reads=[PSN[m], "xcur"],
                   writes=["xcur"], tag=f"{cr}_ra{m}")
        def f_str(e, t0=t0):
            return e.dma_start(
                out=resid.rearrange("(r p) t -> p r t", p=128)
                [:, :, t0:t0 + TC],
                in_=xcur[:].rearrange("p (r t) -> p r t", t=TC)) \
                .then_inc(CURSEM[0], 16)
        P.step("sync", f_str, reads=["xcur"], writes=["resid"],
               inc=16, is_async=True, tag=f"{cr}_str")

    emit_ldrf32(0, 0)
    emit_B(0, 0)
    emit_C(0, 0)

    last_yg = None
    for l in range(NL):
        lr = f"L{l}"
        prevcc = ccdma_steps.get(l - 1, st_ccinit)
        st1 = P.step("vector", (lambda l=l: lambda e: e.tensor_scalar_mul(
            out=state[:], in0=ccres[:, 0:256], scalar1=ccol(OFF_SEL)))(),
            reads=["cols", "flags"], writes=["state"],
            deps=[prevcc], tag=f"{lr}_stinit")
        gate_patches.append((st1, l))
        st2 = P.step("vector", (lambda l=l: lambda e: e.tensor_scalar_mul(
            out=tailbuf[:], in0=ccres[:, 256:304], scalar1=ccol(OFF_SEL)))(),
            reads=["cols", "flags"], writes=["tailbuf"],
            deps=[prevcc], tag=f"{lr}_hinit")
        gate_patches.append((st2, l))
        init_steps[l] = (st1, st2)
        emit_F0(l, 0)

        for ch in range(NCH):
            g = l * NCH + ch
            prep_refill(l, ch)
            emit_abgrp(l, ch, 0)
            sliced = False
            if g + 1 < NG:
                l2, c2 = (g + 1) // NCH, (g + 1) % NCH
                emit_B(l2, c2)
                emit_C(l2, c2)
                if c2 != 0:
                    sliced = True
                    emit_happ(l2, c2)
            if g >= 1:
                l1, c1 = (g - 1) // NCH, (g - 1) % NCH
                emit_E_mm(l1, c1)
            for jb in range(NJ):
                last_yg = emit_Dj(l, ch, jb)
                prep_emit_some(3)
                if jb == 2 and g >= 1:
                    emit_E_fin(l1, c1)
                    emit_ldrf32(l, ch)
                if sliced:
                    if jb == 3:
                        for j2 in range(4):
                            emit_conv_slice(l2, c2, j2)
                    elif jb >= 4:
                        emit_conv_slice(l2, c2, jb)
                    if jb in (3, 7, 11):
                        emit_zgrp(l2, c2, (jb - 3) // 4)
            if sliced:
                emit_zgrp(l2, c2, 3)
                emit_Ftail(l2, c2)

        stg1 = P.step("gpsimd", lambda e: e.dma_start(
            out=cc_in[:, 0:256], in_=state[:]).then_inc(CURSEM[0], 16),
            reads=[], writes=["cc_in"], no_raw=True, inc=16, is_async=True,
            deps=[P.writer.get("cc_out"), st_ccinit, st_haloclr],
            tag=f"{lr}_stg1")
        stg1.reg_wait = ("vector", last_yg, 1)
        stg2 = P.step("gpsimd", lambda e: e.dma_start(
            out=cc_in[:, 256:304].rearrange("p (j w) -> p j w", w=3),
            in_=tailbuf[:].rearrange("p (j w) -> p j w", w=3))
            .then_inc(CURSEM[0], 16),
            reads=[], writes=["cc_in"], no_raw=True, inc=16, is_async=True,
            deps=[st_ccinit, st_haloclr], tag=f"{lr}_stg2")
        stg2.reg_wait = ("vector", last_yg, 1)
        P.readers.setdefault("state", []).append(stg1.idx)
        P.readers.setdefault("tailbuf", []).append(stg2.idx)
        ccstep = P.step("gpsimd", lambda e: e.collective_compute(
            "AllGather", OP.bypass,
            replica_groups=groups or [[0, 1], [2, 3], [4, 5], [6, 7]],
            ins=[cc_in], outs=[cc_out]).then_inc(sems["gpsimd"], 1),
            reads=["cc_in"], writes=["cc_out"], inc=1, is_async=True,
            tag=f"{lr}_cc")
        ccd = P.step("gpsimd", (lambda l=l: lambda e: e.dma_start(
            out=ccres[:], in_=cc_out[0]).then_inc(CURSEM[0], 16))(),
            reads=["cc_out"], inc=16, is_async=True,
            deps=(list(init_steps[l - 1]) if (l - 1) in init_steps else []),
            tag=f"{lr}_ccd")
        ccdma_steps[l] = ccd

    prep_drain()
    emit_E_mm(NL - 1, NCH - 1)
    emit_E_fin(NL - 1, NCH - 1)

    for st, l in gate_patches:
        if l in ccdma_steps:
            st.reg_wait = ("gpsimd", ccdma_steps[l], 0)

    out_dmas = []
    if do_ffn:
        for ch in range(NCH):
            cr = f"Fc{ch}"
            t0 = ch * TC
            def f_ldr(e, t0=t0):
                return e.dma_start(
                    out=xcur[:].rearrange("p (r t) -> p r t", t=TC),
                    in_=resid.rearrange("(r p) t -> p r t", p=128)
                    [:, :, t0:t0 + TC]).then_inc(CURSEM[0], 16)
            P.step("sync", f_ldr, reads=["resid"], writes=["xcur"], inc=16,
                   is_async=True, tag=f"{cr}_ldr")
            def f_ldx(e, t0=t0):
                return e.dma_start(
                    out=xorig[:, 0:ND * TC].rearrange("p (r t) -> p r t", t=TC),
                    in_=xin.rearrange("(r p) t -> p r t", p=128)
                    [:, :, t0:t0 + TC]).then_inc(CURSEM[0], 16)
            P.step("sync", f_ldx, writes=["crep"], inc=16, is_async=True,
                   tag=f"{cr}_ldx")
            def f_ldm(e, t0=t0):
                return e.dma_start(
                    out=maskv[:, 0:ND * TC].rearrange("p (r t) -> p r t", t=TC),
                    in_=maskin.rearrange("(r p) t -> p r t", p=128)
                    [:, :, t0:t0 + TC]).then_inc(CURSEM[0], 16)
            P.step("sync", f_ldm, writes=["brep"], inc=16, is_async=True,
                   tag=f"{cr}_ldm")

            P.step("scalar", lambda e: e.activation(
                out=xsq[:, 0:ND * TC], in_=xcur[:, 0:ND * TC], func=AF.Square),
                reads=["xcur"], writes=["xcr"], tag=f"{cr}_fsq")
            stats_ssq("xcr", xsq, "psF", f"{cr}_fss")
            rms_inv("psF", epsrms[:], f"{cr}_finv")
            bcast("stati", stati[:], "psH", f"{cr}_fbi")
            for r in range(ND):
                def f_so(e, r=r):
                    return e.scalar_tensor_tensor(
                        out=sl(xsq, r), in0=sl(xcur, r),
                        scalar=ccol(OFF_FINW + r), in1=ps["psH"][:],
                        op0=OP.mult, op1=OP.mult)
                P.step("vector", f_so, reads=["xcur", "psH", "cols"],
                       writes=["xcr"], tag=f"{cr}_so{r}")
            for r in range(ND):
                def f_xo(e, r=r):
                    return e.scalar_tensor_tensor(
                        out=sl(xcur, r), in0=sl(xsq, r),
                        scalar=dco(DC_GA + r), in1=sl(xorig, r),
                        op0=OP.mult, op1=OP.add)
                P.step("vector", f_xo, reads=["xcr", "crep", "dcol"],
                       writes=["xcur"], tag=f"{cr}_xo{r}")

            P.step("scalar", lambda e: e.activation(
                out=xsq[:, 0:ND * TC], in_=xcur[:, 0:ND * TC], func=AF.Square),
                reads=["xcur"], writes=["xcr"], tag=f"{cr}_2sq")
            stats_ssq("xcur", xcur, "psE", f"{cr}_2sx")
            stats_ssq("xcr", xsq, "psF", f"{cr}_2sxx")
            def f_rows2(e):
                e.tensor_scalar_mul(out=statm[:], in0=ps["psE"][0:1, :],
                                    scalar1=1.0 / D)
                e.tensor_mul(out=statv, in0=statm[:], in1=statm[:])
                return e.scalar_tensor_tensor(
                    out=statv, in0=ps["psF"][0:1, :],
                    scalar=1.0 / D, in1=statv,
                    op0=OP.mult, op1=OP.subtract)
            P.step("vector", f_rows2, reads=["psE", "psF"],
                   writes=["statm", "ytot"], tag=f"{cr}_2rows")
            P.step("scalar", lambda e: e.activation(
                out=statv, in_=statv, func=AF.Sqrt, bias=epsln[:]),
                reads=["ytot", "consts"], writes=["ytot"], tag=f"{cr}_2invs")
            P.step("vector", lambda e: e.reciprocal(
                out=stati[:], in_=statv), reads=["ytot"],
                writes=["stati"], tag=f"{cr}_2invr")
            bcast("statm", statm[:], "psG", f"{cr}_2bm")
            bcast("stati", stati[:], "psH", f"{cr}_2bi")
            for r in range(ND):
                def f_q(e, r=r):
                    e.tensor_sub(out=sl(xsq, r), in0=sl(xcur, r),
                                 in1=ps["psG"][:])
                    e.tensor_mul(out=sl(xsq, r), in0=sl(xsq, r),
                                 in1=ps["psH"][:])
                    return e.tensor_scalar(
                        out=sl(xqv, r), in0=sl(xsq, r),
                        scalar1=dco(DC_ONEPCB + r), scalar2=dco(DC_MOD + 24 + r),
                        op0=OP.mult, op1=OP.add)
                P.step("vector", f_q, reads=["xcur", "psG", "psH", "dcol"],
                       writes=["szb", "xcr"], tag=f"{cr}_q{r}")

            for k in range(8):
                vw = wf8a if k < 4 else wf8b
                def f_w1(e, k=k, vw=vw):
                    return e.dma_start(
                        out=vw[:, (k % 4) * 4096:(k % 4) * 4096 + 4096],
                        in_=wS1[k]).then_inc(CURSEM[0], 16)
                P.step("sync", f_w1,
                       writes=[f"wf{k}", "xcs" if k < 4 else "xcr"],
                       inc=16, is_async=True,
                       deps=[w1_rdy(k)], tag=f"{cr}_w1l{k}")
            for m in range(32):
                pA = PSN[m % 8]
                for k in range(8):
                    vw = wf8a if k < 4 else wf8b
                    def f_m1(e, m=m, k=k, vw=vw, pA=pA):
                        return e.matmul(
                            ps[pA][:],
                            vw[:, (k % 4) * 4096 + m * 128:
                               (k % 4) * 4096 + (m + 1) * 128],
                            xqv[:, k * TC:(k + 1) * TC],
                            start=(k == 0), stop=(k == 7))
                    P.step("tensor", f_m1, reads=[f"wf{k}", "szb"],
                           writes=[pA], tag=f"{cr}_m1{m}_{k}")
                hf = hflo if m < 16 else hfhi
                def f_e1(e, m=m, pA=pA, hf=hf):
                    return e.activation(
                        out=hf[:, (m % 16) * TC:(m % 16 + 1) * TC],
                        in_=ps[pA][:], func=AF.Relu,
                        bias=dco(DC_B1Q + m))
                P.step("scalar", f_e1, reads=[pA, "dcol"],
                       writes=["hpool" if m < 16 else "hcpool"],
                       deps=([last_prep[0]] if (ch == 0 and m == 0) else []),
                       tag=f"{cr}_e1{m}")
            def f_c8(e):
                e.tensor_copy(out=h8v[:, 0:8192], in_=hflo[:, 0:8192])
                return e.tensor_copy(out=h8v[:, 8192:16384],
                                     in_=hfhi[:, 0:8192])
            P.step("vector", f_c8, reads=["hpool", "hcpool"], writes=["u_s"],
                   tag=f"{cr}_c8")

            for k in range(32):
                def f_w2(e, k=k):
                    return e.dma_start(out=wo8[k % 2][:, 0:1024],
                                       in_=wS2[k]).then_inc(CURSEM[0], 16)
                P.step("sync", f_w2, writes=[f"wo{k % 2}"], inc=16,
                       is_async=True, deps=[w2_rdy(k)], tag=f"{cr}_w2l{k}")
                for m in range(8):
                    def f_m2(e, k=k, m=m):
                        return e.matmul(
                            ps[PSN[m]][:],
                            wo8[k % 2][:, m * 128:(m + 1) * 128],
                            h8v[:, k * TC:(k + 1) * TC],
                            start=(k == 0), stop=(k == 31))
                    P.step("tensor", f_m2, reads=[f"wo{k % 2}", "u_s"],
                           writes=[PSN[m]], tag=f"{cr}_m2{k}_{m}")
            for m in range(8):
                def f_e2(e, m=m):
                    return e.activation(
                        out=y2v[:, m * TC:(m + 1) * TC], in_=ps[PSN[m]][:],
                        func=AF.Identity, bias=dco(DC_B2Q + m))
                P.step("scalar", f_e2, reads=[PSN[m], "dcol"], writes=["dlt"],
                       tag=f"{cr}_e2{m}")

            for r in range(ND):
                def f_mk(e, r=r):
                    return e.tensor_mul(out=sl(xsq, r),
                                        in0=y2v[:, r * TC:(r + 1) * TC],
                                        in1=maskv[:, r * TC:(r + 1) * TC])
                P.step("vector", f_mk, reads=["dlt", "brep"],
                       writes=["xcr"], tag=f"{cr}_mk{r}")
                def f_fin(e, r=r):
                    return e.scalar_tensor_tensor(
                        out=sl(xcur, r), in0=sl(xsq, r),
                        scalar=dco(DC_GB + r), in1=sl(xcur, r),
                        op0=OP.mult, op1=OP.add)
                P.step("vector", f_fin, reads=["xcr", "dcol", "xcur"],
                       writes=["xcur"], tag=f"{cr}_fin{r}")
            st_out = P.step("gpsimd", (lambda t0=t0: lambda e: e.dma_start(
                out=xout.rearrange("(r p) t -> p r t", p=128)[:, :, t0:t0 + TC],
                in_=xcur[:].rearrange("p (r t) -> p r t", t=TC))
                .then_inc(CURSEM[0], 16))(),
                reads=["xcur"], writes=["xout"], inc=16, is_async=True,
                tag=f"{cr}_out")
            out_dmas.append(st_out)
    else:
        for ch in range(NCH):
            t0 = ch * TC
            def f_cp(e, t0=t0):
                return e.dma_start(
                    out=xout.rearrange("(r p) t -> p r t", p=128)
                    [:, :, t0:t0 + TC],
                    in_=resid.rearrange("(r p) t -> p r t", p=128)
                    [:, :, t0:t0 + TC]).then_inc(CURSEM[0], 16)
            st_out = P.step("gpsimd", f_cp, reads=["resid"], writes=["xout"],
                            inc=16, is_async=True, tag=f"cp{ch}")
            out_dmas.append(st_out)

    if dbg is not None:
        dmap = {
            "xq": (szb[:].bitcast(F8)[:, 0:4096], "szb"),
            "h8": (u_s[:].bitcast(F8)[:, 0:16384], "u_s"),
            "y2": (dlt[:].bitcast(F16)[:, 0:4096], "dlt"),
            "xcur": (xcur[:], "xcur"),
            "slotout": (hpool[:].bitcast(F32)[:, 0:4096], "hpool"),
            "ytot": (ytot[:], "ytot"),
            "szb": (szb[:], "szb"),
            "dlt": (dlt[:], "dlt"),
            "xcs": (xcs[:], "xcs"),
            "dcol": (dcol[:], "dcol"),
            "apos": (apos[:], "apos"),
        }
        src_ap, bufname = dmap[dbg[0]]
        st_d = P.step("gpsimd", lambda e: e.dma_start(out=dbg_out, in_=src_ap)
                      .then_inc(CURSEM[0], 16), reads=[bufname],
                      writes=["dbgout"], inc=16, is_async=True, tag="dbg")
        out_dmas.append(st_d)

    fin = P.step("gpsimd", lambda e: e.engine_nop(),
                 deps=[s.idx for s in out_dmas], tag="final")

    P.finalize()
    if os.environ.get("KDUMP"):
        with open("/tmp/steps.txt", "w") as fh:
            for s in P.steps:
                fh.write(f"{s.idx} {s.eng} {s.sem} {s.sem_val} {s.tag} "
                         f"deps={sorted(s.deps)}\n")

    eng_steps = {e: [s for s in P.steps if s.eng == e] for e in Prog.ENGS}

    def emit(eng_name, eng):
        seen = {e: 0 for e in semnames}
        for s in eng_steps[eng_name]:
            for d in sorted(s.deps):
                dst = P.steps[d]
                if dst.eng == eng_name and not dst.is_async:
                    continue
                if dst.sem_val > seen[dst.sem]:
                    eng.wait_ge(sems[dst.sem], dst.sem_val)
                    seen[dst.sem] = dst.sem_val
            if s.reg_wait is not None:
                weng, wstep, fidx = s.reg_wait
                fl = eng.value_load(flags[0:1, fidx:fidx + 1])
                tgt = eng.compute_val(fl * int(wstep.sem_val))
                eng.wait_ge(sems[wstep.sem], tgt)
            if s.is_async and s.sem not in Prog.ENGS:
                prev = s.sem_val - s.inc
                if prev > seen[s.sem]:
                    eng.wait_ge(sems[s.sem], prev)
                    seen[s.sem] = prev
            CURSEM[0] = sems[s.sem]
            inst = s.fn(eng)
            if inst is not None and s.inc == 1 and not s.is_async:
                inst.then_inc(sems[s.sem], 1)

    with nc.Block() as block:
        @block.sync
        def _(e):
            emit("sync", e)

        @block.vector
        def _(e):
            emit("vector", e)

        @block.scalar
        def _(e):
            emit("scalar", e)

        @block.tensor
        def _(e):
            emit("tensor", e)

        @block.gpsimd
        def _(e):
            emit("gpsimd", e)

    es.close()
    nc.finalize()
    return nc


def _prep_inputs(inputs):
    import jax
    x = np.ascontiguousarray(np.asarray(inputs["L0_attn_x"], np.float32))
    c = np.asarray(inputs["L0_c"], np.float32)
    wmod = np.asarray(inputs["L0_wmod"], np.float32)
    norm_w = np.asarray(inputs["L0_attn_norm_w"], np.float32)
    in_w = np.asarray(inputs["L0_attn_in_w"], np.float32)
    conv_w = np.asarray(inputs["L0_attn_conv_w"], np.float32)
    conv_b = np.asarray(inputs["L0_attn_conv_b"], np.float32)
    A_log = np.asarray(inputs["L0_attn_A_log"], np.float32)
    Dp = np.asarray(inputs["L0_attn_Dp"], np.float32)
    xproj_w = np.asarray(inputs["L0_attn_xproj_w"], np.float32)
    dt_w = np.asarray(inputs["L0_attn_dt_w"], np.float32)
    dt_b = np.asarray(inputs["L0_attn_dt_b"], np.float32)
    out_w = np.asarray(inputs["L0_attn_out_w"], np.float32)
    final_w = np.asarray(inputs["L0_attn_final_w"], np.float32)
    w1 = np.asarray(inputs["L0_ffn_w1"], np.float32)
    b1 = np.asarray(inputs["L0_ffn_b1"], np.float32)
    w2 = np.asarray(inputs["L0_ffn_w2"], np.float32)
    b2 = np.asarray(inputs["L0_ffn_b2"], np.float32)

    with jax.default_device(jax.devices("cpu")[0]):
        with jax.default_prng_impl("threefry2x32"):
            keep = np.asarray(jax.random.bernoulli(
                jax.random.key(42), 0.8, (4, 4096, D)))
    maskf = (keep.astype(np.float32) * 1.25).astype(np.float16)

    w_in = np.ascontiguousarray(
        np.stack([in_w[l].T.reshape(8, 128, 4096) for l in range(NL)])
        .reshape(NL * 8, 128, 4096))
    w_out = np.ascontiguousarray(
        np.stack([out_w[l].T.reshape(16, 128, 1024) for l in range(NL)])
        .reshape(NL * 16, 128, 1024))
    w_xp = np.ascontiguousarray(
        np.stack([xproj_w[l].T.reshape(16, 128, 96)
                  .transpose(1, 0, 2).reshape(128, 16 * 96)
                  for l in range(NL)]))
    w_dt = np.ascontiguousarray(
        np.stack([dt_w[l].T for l in range(NL)]))
    w1T = np.ascontiguousarray(w1.T.reshape(8, 128, 4096))
    w2T = np.ascontiguousarray(w2.T.reshape(32, 128, 1024))
    wmod_s = np.ascontiguousarray(wmod.reshape(8, 128, 6144))

    idx = np.arange(128)
    wcd = np.zeros((NL, 16, 128, 4, 128), ml_dtypes.bfloat16)
    for l in range(NL):
        cw = conv_w[l][:, 0, :].reshape(16, 128, 4)
        for j in range(16):
            for k in range(4):
                wcd[l, j, idx, k, idx] = cw[j, :, k].astype(
                    ml_dtypes.bfloat16)
    wconvd = np.ascontiguousarray(wcd.reshape(NL * 16, 128, 512))

    alog = np.ascontiguousarray(np.concatenate(
        [A_log[l].reshape(16, 128, 16).transpose(1, 0, 2).reshape(128, 256)
         for l in range(NL)], axis=1))

    def colpack(b):
        colsx = np.zeros((128, NCOLS), np.float32)
        for l in range(NL):
            cw = conv_w[l][:, 0, :].reshape(16, 128, 4)
            colsx[:, OFF_CONVW + l * 64:OFF_CONVW + (l + 1) * 64] = \
                cw.transpose(1, 0, 2).reshape(128, 64)
            colsx[:, OFF_CONVB + l * 16:OFF_CONVB + (l + 1) * 16] = \
                conv_b[l].reshape(16, 128).T
            colsx[:, OFF_DP + l * 16:OFF_DP + (l + 1) * 16] = \
                Dp[l].reshape(16, 128).T
            colsx[:, OFF_DTB + l * 16:OFF_DTB + (l + 1) * 16] = \
                dt_b[l].reshape(16, 128).T
            colsx[:, OFF_NORMW + l * 8:OFF_NORMW + (l + 1) * 8] = \
                norm_w[l].reshape(8, 128).T
        colsx[:, OFF_FINW:OFF_FINW + 8] = final_w.reshape(8, 128).T
        colsx[:, OFF_C:OFF_C + 8] = c[b].reshape(8, 128).T
        colsx[:, OFF_B1:OFF_B1 + 32] = b1.reshape(32, 128).T
        colsx[:, OFF_B2:OFF_B2 + 8] = b2.reshape(8, 128).T
        return colsx

    in_maps = []
    for core in range(8):
        b, h = core >> 1, core & 1
        colsx = colpack(b).copy()
        colsx[:, OFF_SEL] = float(h)
        xf = np.ascontiguousarray(x[b].T[:, h * LC:(h + 1) * LC])
        mf = np.ascontiguousarray(maskf[b].T[:, h * LC:(h + 1) * LC])
        in_maps.append({
            "xin": xf, "maskin": mf, "cols": colsx, "alog": alog,
            "id8": np.eye(128, dtype=ml_dtypes.float8_e4m3),
            "flags": np.array([[h, 1 - h]], np.int32),
            "w_in": w_in, "w_out": w_out, "w_xp": w_xp, "w_dt": w_dt,
            "w1": w1T, "w2": w2T, "wmod": wmod_s, "wconvd": wconvd,
        })
    return in_maps


_NC_CACHE = {}


def kernel(**inputs):
    key = ("full",)
    if key not in _NC_CACHE:
        _NC_CACHE[key] = build()
    nc = _NC_CACHE[key]
    in_maps = _prep_inputs(inputs)
    res = run_bass_kernel_spmd(nc, in_maps, core_ids=list(range(8)))
    out = np.zeros((4, 4096, D), np.float32)
    for core in range(8):
        b, h = core >> 1, core & 1
        out[b, h * LC:(h + 1) * LC, :] = res.results[core]["xout"].T
    return out


# version 11
# speedup vs baseline: 1.0902x; 1.0902x over previous
import os
import numpy as np
import ml_dtypes
import concourse.bass as bass
import concourse.mybir as mybir
from concourse.bass_utils import run_bass_kernel_spmd
from contextlib import ExitStack

F32 = mybir.dt.float32
BF16 = mybir.dt.bfloat16
F16 = mybir.dt.float16
F8 = mybir.dt.float8e4
I32 = mybir.dt.int32
AF = mybir.ActivationFunctionType
OP = mybir.AluOpType

D, DI, NL, NST, RR = 1024, 2048, 4, 16, 64
LC, TC = 2048, 512
NCH = LC // TC
ND = D // 128
NJ = DI // 128
XCW = TC + 3

OFF_CONVW, OFF_CONVB, OFF_DP, OFF_DTB = 0, 256, 320, 384
OFF_NORMW, OFF_FINW, OFF_C = 448, 480, 488
OFF_B1, OFF_B2, OFF_SEL = 496, 528, 536
NCOLS = 544

DC_NEGDTB, DC_SILUC, DC_MOD = 0, 64, 72
DC_ONEPCA, DC_ONEPCB, DC_GA, DC_GB = 120, 128, 136, 144
DC_B1Q, DC_B2Q = 152, 184
DCW = 192


class Step:
    __slots__ = ("idx", "eng", "fn", "deps", "inc", "sem_val", "is_async",
                 "reg_wait", "tag", "sem")

    def __init__(self, idx, eng, fn, deps, inc, is_async, tag):
        self.idx, self.eng, self.fn = idx, eng, fn
        self.deps, self.inc, self.is_async = deps, inc, is_async
        self.reg_wait = None
        self.tag = tag
        self.sem = None


class Prog:
    ENGS = ("sync", "vector", "scalar", "tensor", "gpsimd")
    NSY, NGP = 6, 4

    def __init__(self):
        self.steps = []
        self.writer = {}
        self.readers = {}
        self._syc = 0
        self._gpc = 0

    def step(self, eng, fn, reads=(), writes=(), inc=1, deps=(),
             is_async=False, no_raw=False, tag=""):
        idx = len(self.steps)
        dep = set()
        for d in deps:
            if d is not None:
                dep.add(d if isinstance(d, int) else d.idx)
        if not no_raw:
            for b in reads:
                if b in self.writer:
                    dep.add(self.writer[b])
        for b in writes:
            if b in self.writer:
                dep.add(self.writer[b])
            for r in self.readers.get(b, ()):
                dep.add(r)
        st = Step(idx, eng, fn, dep, inc, is_async, tag)
        if eng == "sync":
            st.sem = f"sync{self._syc % self.NSY}"
            self._syc += 1
        elif eng == "gpsimd" and inc % 16 == 0 and inc > 0:
            st.sem = f"gpd{self._gpc % self.NGP}"
            self._gpc += 1
        else:
            st.sem = eng
        self.steps.append(st)
        for b in reads:
            self.readers.setdefault(b, []).append(idx)
        for b in writes:
            self.writer[b] = idx
            self.readers[b] = []
        return st

    def finalize(self):
        cnt = {}
        for st in self.steps:
            cnt[st.sem] = cnt.get(st.sem, 0) + st.inc
            st.sem_val = cnt[st.sem]
        self.totals = cnt


def build(n_layers=NL, do_ffn=True, dbg=None, groups=None):
    nc = bass.Bass("TRN2", target_bir_lowering=False, debug=False,
                   detect_race_conditions=False)

    def din(name, shape, dt=F32):
        return nc.dram_tensor(name, list(shape), dt, kind="ExternalInput").ap()

    def dint(name, shape, dt):
        return nc.dram_tensor(name, list(shape), dt, kind="Internal").ap()

    xin = din("xin", [D, LC])
    maskin = din("maskin", [D, LC], F16)
    cols_d = din("cols", [128, NCOLS])
    alog_d = din("alog", [128, NL * 256])
    id8_d = din("id8", [128, 128], F8)
    flags_d = din("flags", [1, 2], I32)
    w_in_d = din("w_in", [NL * 8, 128, 4096])
    w_out_d = din("w_out", [NL * 16, 128, 1024])
    w_xp_d = din("w_xp", [NL, 128, 16 * 96])
    w_dt_d = din("w_dt", [NL, 64, 2048])
    w1_d = din("w1", [8, 128, 4096])
    w2_d = din("w2", [32, 128, 1024])
    wmod_d = din("wmod", [8, 128, 6144])
    wcd_d = din("wconvd", [NL * 16, 128, 512], BF16)

    xout = nc.dram_tensor("xout", [D, LC], F32, kind="ExternalOutput").ap()
    dbg_out = None
    if dbg is not None:
        dbg_out = nc.dram_tensor("dbg", [128, dbg[1]], dbg[2],
                                 kind="ExternalOutput").ap()

    resid = dint("resid", [D, LC], F32)
    wSin = dint("wSin", [NL * 8, 128, 4096], BF16)
    wSout = dint("wSout", [NL * 16, 128, 1024], BF16)
    wSxp = dint("wSxp", [NL, 128, 16 * 96], BF16)
    wSdt = dint("wSdt", [NL, 64, 2048], BF16)
    wS1 = dint("wS1", [8, 128, 4096], F8)
    wS2 = dint("wS2", [32, 128, 1024], F8)
    modscr = dint("modscr", [1, 6144], F32)
    bcscr = dint("bcscr", [32, TC], BF16)
    cc_in = dint("cc_in", [128, 304], F32)
    cc_out = dint("cc_out", [2, 128, 304], F32)

    es = ExitStack()

    def sbt(name, shape, dt=F32):
        return es.enter_context(nc.sbuf_tensor(name, shape, dt))

    cols = sbt("cols_s", [128, NCOLS])
    apos = sbt("apos", [128, NL * 256])
    dcol = sbt("dcol", [128, DCW])
    sgn = sbt("sgn", [128, 16])
    id8_s = sbt("id8_s", [128, 128], F8)
    bq_s = sbt("bq_s", [128, 40], F8)
    flags = sbt("flags_s", [1, 2], I32)
    ones_c = sbt("ones_c", [128, 1])
    ones_cb = sbt("ones_cb", [128, 1], BF16)
    ones_rf = sbt("ones_rf", [1, 128])
    epsln = sbt("epsln", [1, 1])
    epsrms = sbt("epsrms", [1, 1])
    eps37 = sbt("eps37", [128, 1])
    statm = sbt("statm", [1, TC])
    stati = sbt("stati", [1, TC])
    xcur = sbt("xcur", [128, ND * TC])
    xcr = sbt("xcr", [128, NJ * XCW], BF16)
    xcs = sbt("xcs", [128, NJ * TC], BF16)
    szb = sbt("szb", [128, NJ * TC], BF16)
    dlt = sbt("dlt", [128, NJ * TC], BF16)
    u_s = sbt("u_s", [128, NJ * TC], BF16)
    brep = sbt("brep", [128, NST * TC], BF16)
    crep = sbt("crep", [128, NST * TC], BF16)
    hpool = sbt("hpool", [128, NST * TC], BF16)
    hcpool = sbt("hcpool", [128, NST * TC], BF16)
    ytot = sbt("ytot", [128, NJ * TC], BF16)
    abar = [sbt("abar0", [128, TC], BF16), sbt("abar1", [128, TC], BF16)]
    statv = ytot[:].bitcast(F32)[0:1, 0:TC]
    bxb = sbt("bxb", [128, TC], BF16)
    xpb = sbt("xpb", [96, TC], BF16)
    state = sbt("state", [128, 256])
    tailbuf = sbt("tailbuf", [128, 48], BF16)
    ccres = sbt("ccres", [128, 304])
    wi = [sbt(f"wi{k}", [128, 512], BF16) for k in range(8)]
    wo = [sbt("wo0", [128, 1024], BF16), sbt("wo1", [128, 1024], BF16)]
    wxpt = [sbt("wxpt0", [128, 96], BF16), sbt("wxpt1", [128, 96], BF16)]
    wdtt = [sbt("wdtt0", [64, 128], BF16), sbt("wdtt1", [64, 128], BF16)]

    xhat = hpool[:, 0:4096]
    xorig = crep[:].bitcast(F32)
    xsq = xcr[:].bitcast(F32)[:, 0:4096]
    xf32 = xcr[:].bitcast(F32)[:, 0:4096]
    h8v = u_s[:].bitcast(F8)
    xqv = szb[:].bitcast(F8)
    hflo = hpool[:].bitcast(F16)
    hfhi = hcpool[:].bitcast(F16)
    y2v = dlt[:].bitcast(F16)
    maskv = brep[:].bitcast(F16)
    wf8a = xcs[:].bitcast(F8)
    wf8b = xcr[:].bitcast(F8)
    wo8 = [w[:].bitcast(F8) for w in wo]
    prep_in = [hpool[:].bitcast(F32), hcpool[:].bitcast(F32)]
    prep_out = [ytot[:, 0:4096], ytot[:, 4096:8192]]
    prep_out8 = ytot[:].bitcast(F8)


    ps = {n: es.enter_context(nc.psum_tensor(n, [128, TC], F32))
          for n in ("psA", "psB", "psC", "psD", "psE", "psF", "psG", "psH")}
    PSN = list(ps)

    semnames = list(Prog.ENGS) + [f"sync{i}" for i in range(Prog.NSY)] \
        + [f"gpd{i}" for i in range(Prog.NGP)]
    sems = {e: es.enter_context(nc.semaphore("sem_" + e)) for e in semnames}

    P = Prog()
    CURSEM = [None]

    def ccol(off):
        return cols[:, off:off + 1]

    def dco(off):
        return dcol[:, off:off + 1]

    def sl(buf, j, w=TC):
        return buf[:, j * w:(j + 1) * w]

    P.step("sync", lambda e: e.dma_start(out=cols[:], in_=cols_d)
           .then_inc(CURSEM[0], 16), writes=["cols"], inc=16,
           is_async=True, tag="ld_cols")
    P.step("sync", lambda e: e.dma_start(out=flags[:], in_=flags_d)
           .then_inc(CURSEM[0], 16), writes=["flags"], inc=16,
           is_async=True, tag="ld_flags")

    def init_consts(e):
        e.memset(ones_c[:], 1.0)
        e.memset(ones_cb[:], 1.0)
        e.memset(ones_rf[:], 1.0)
        e.memset(epsln[:], 1e-5)
        e.memset(epsrms[:], 1e-6)
        e.memset(eps37[:], 1e-37)
        e.memset(state[:], 0.0)
        e.memset(ccres[:], 0.0)
        e.memset(xcr[:], 0.0)
        return e.engine_nop()
    st_ccinit = P.step("vector", init_consts,
                       writes=["state", "consts", "xcr"], tag="init")

    P.step("sync", lambda e: e.dma_start(
        out=xcur[:].bitcast(F32)[:, 1024:2048], in_=alog_d)
        .then_inc(CURSEM[0], 16), writes=["xcur"], inc=16, is_async=True,
        tag="ld_alog")
    P.step("scalar", lambda e: e.activation(
        out=apos[:], in_=xcur[:].bitcast(F32)[:, 1024:2048], func=AF.Exp),
        reads=["xcur"], writes=["apos"], tag="apos")
    P.step("vector", lambda e: e.tensor_scalar_mul(
        out=dcol[:, DC_NEGDTB:DC_NEGDTB + 64],
        in0=cols[:, OFF_DTB:OFF_DTB + 64], scalar1=-1.0),
        reads=["cols"], writes=["dcol"], tag="negdtb")
    P.step("sync", lambda e: e.dma_start(out=id8_s[:], in_=id8_d)
           .then_inc(CURSEM[0], 16), writes=["id8"], inc=16, is_async=True,
           tag="ld_id8")
    P.step("vector", lambda e: e.tensor_copy(
        out=bq_s[:], in_=cols[:, OFF_B1:OFF_B1 + 40]),
        reads=["cols"], writes=["bq_s"], tag="bq8")
    P.step("tensor", lambda e: e.matmul(
        ps["psE"][:, 0:40], id8_s[:], bq_s[:], start=True, stop=True),
        reads=["bq_s", "id8"], writes=["psE"], tag="bqmm")
    P.step("scalar", lambda e: e.copy(
        out=dcol[:, DC_B1Q:DC_B1Q + 40], in_=ps["psE"][:, 0:40]),
        reads=["psE"], writes=["dcol"], tag="bqf")
    P.step("scalar", lambda e: e.activation(
        out=dcol[:, DC_SILUC:DC_SILUC + 8], in_=cols[:, OFF_C:OFF_C + 8],
        func=AF.Silu), reads=["cols"], writes=["dcol"], tag="siluc")

    prep_specs = []
    for l in range(NL):
        for k in range(8):
            prep_specs.append((("win", l, k), w_in_d[l * 8 + k],
                               wSin[l * 8 + k], 128, 4096, BF16))
        prep_specs.append((("wxp", l), w_xp_d[l], wSxp[l], 128, 16 * 96, BF16))
        prep_specs.append((("wdt", l), w_dt_d[l], wSdt[l], 64, 2048, BF16))
        for k in range(16):
            prep_specs.append((("wout", l, k), w_out_d[l * 16 + k],
                               wSout[l * 16 + k], 128, 1024, BF16))
    for k in range(8):
        prep_specs.append((("w1", k), w1_d[k], wS1[k], 128, 4096, F8))
    for k in range(32):
        prep_specs.append((("w2", k), w2_d[k], wS2[k], 128, 1024, F8))

    prep_rdy = {}
    for i, (pkey, src, dst, prows, width, odt) in enumerate(prep_specs):
        par = i % 2
        pin = prep_in[par][0:prows, 0:width]
        if odt == BF16:
            pout = prep_out[par][0:prows, 0:width]
        else:
            pout = prep_out8[0:prows, par * 8192:par * 8192 + width]

        def f_ld(e, src=src, pin=pin):
            return e.dma_start(out=pin, in_=src).then_inc(CURSEM[0], 16)

        def f_cast(e, pin=pin, pout=pout):
            return e.tensor_copy(out=pout, in_=pin)

        def f_st(e, dst=dst, pout=pout):
            return e.dma_start(out=dst, in_=pout).then_inc(CURSEM[0], 16)

        inbuf = "hpool" if par == 0 else "hcpool"
        P.step("sync", f_ld, writes=[inbuf], inc=16, is_async=True,
               tag=f"prep_ld{i}")
        P.step("vector", f_cast, reads=[inbuf], writes=["ytot"],
               tag=f"prep_cast{i}")
        st_prep = P.step("gpsimd", f_st, reads=["ytot"],
                         writes=[f"wS{i}"], inc=16, is_async=True,
                         tag=f"prep_st{i}")
        prep_rdy[pkey] = st_prep

    prep_hp_guard = [P.writer.get("hpool")] + list(P.readers.get("hpool", ()))

    def win_rdy(l, k):
        return prep_rdy[("win", l, k)]
    def wout_rdy(l, k):
        return prep_rdy[("wout", l, k)]
    def wxp_rdy(l):
        return prep_rdy[("wxp", l)]
    def wdt_rdy(l):
        return prep_rdy[("wdt", l)]
    def w1_rdy(k):
        return prep_rdy[("w1", k)]
    def w2_rdy(k):
        return prep_rdy[("w2", k)]

    wmbf32 = xcr[:].bitcast(F32)
    for grp in range(2):
        for s in range(8):
            def f_wm(e, s=s, grp=grp):
                return e.dma_start(
                    out=wmbf32[:, 0:3072],
                    in_=wmod_d[s, :, grp * 3072:(grp + 1) * 3072]) \
                    .then_inc(CURSEM[0], 16)
            P.step("sync", f_wm, reads=[], writes=["xcr"], inc=16,
                   is_async=True, tag=f"modw{grp}{s}")
            for p in range(6):
                def f_mm(e, s=s, p=p):
                    return e.matmul(
                        ps[PSN[p]][0:1, :],
                        dcol[:, DC_SILUC + s:DC_SILUC + s + 1],
                        wmbf32[:, p * TC:(p + 1) * TC],
                        start=(s == 0), stop=(s == 7))
                P.step("tensor", f_mm, reads=["xcr", "dcol"],
                       writes=[f"{PSN[p]}"], tag=f"modmm{grp}{s}{p}")
        for p in range(6):
            def f_ev(e, p=p):
                return e.copy(out=statm[:], in_=ps[PSN[p]][0:1, :])
            P.step("scalar", f_ev, reads=[f"{PSN[p]}"], writes=["statm"],
                   tag=f"modev{grp}{p}")
            pi = grp * 6 + p
            def f_st(e, pi=pi):
                return e.dma_start(
                    out=modscr[0:1, pi * TC:(pi + 1) * TC], in_=statm[:]) \
                    .then_inc(CURSEM[0], 16)
            P.step("gpsimd", f_st, reads=["statm"], writes=["modscr"],
                   inc=16, is_async=True, tag=f"modst{grp}{p}")
    P.step("sync", lambda e: e.dma_start(
        out=dcol[:, DC_MOD:DC_MOD + 48],
        in_=modscr[0].rearrange("(b r p) -> p (b r)", b=6, r=8, p=128),
        allow_slow_non_contiguous=True)
        .then_inc(CURSEM[0], 16), reads=["modscr"], writes=["dcol"],
        inc=16, is_async=True, tag="modld")

    def drv0(e):
        e.activation(out=dcol[:, DC_GA:DC_GA + 8],
                     in_=dcol[:, DC_MOD + 16:DC_MOD + 24], func=AF.Abs)
        return e.activation(out=dcol[:, DC_GB:DC_GB + 8],
                            in_=dcol[:, DC_MOD + 40:DC_MOD + 48], func=AF.Abs)
    P.step("scalar", drv0, reads=["dcol"], writes=["dcol"], tag="drv0")

    def drv(e):
        e.tensor_scalar_add(out=dcol[:, DC_ONEPCA:DC_ONEPCA + 8],
                            in0=dcol[:, DC_MOD + 8:DC_MOD + 16], scalar1=1.0)
        e.tensor_scalar_add(out=dcol[:, DC_ONEPCB:DC_ONEPCB + 8],
                            in0=dcol[:, DC_MOD + 32:DC_MOD + 40], scalar1=1.0)
        e.tensor_scalar_max(out=dcol[:, DC_GA:DC_GA + 8],
                            in0=dcol[:, DC_GA:DC_GA + 8], scalar1=0.25)
        return e.tensor_scalar_max(out=dcol[:, DC_GB:DC_GB + 8],
                                   in0=dcol[:, DC_GB:DC_GB + 8], scalar1=0.25)
    P.step("vector", drv, reads=["dcol"], writes=["dcol"], tag="drv")

    def sgnf(e):
        e.activation(out=sgn[:, 0:8], in_=dcol[:, DC_MOD + 16:DC_MOD + 24],
                     func=AF.Sign, bias=eps37[:])
        return e.activation(out=sgn[:, 8:16],
                            in_=dcol[:, DC_MOD + 40:DC_MOD + 48],
                            func=AF.Sign, bias=eps37[:])
    P.step("scalar", sgnf, reads=["dcol", "consts"], writes=["sgn"], tag="sgn")

    def gmul(e):
        e.tensor_mul(out=dcol[:, DC_GA:DC_GA + 8],
                     in0=dcol[:, DC_GA:DC_GA + 8], in1=sgn[:, 0:8])
        return e.tensor_mul(out=dcol[:, DC_GB:DC_GB + 8],
                            in0=dcol[:, DC_GB:DC_GB + 8], in1=sgn[:, 8:16])
    P.step("vector", gmul, reads=["sgn", "dcol"], writes=["dcol"], tag="gmul")

    XCRS = [f"xcr{j}" for j in range(NJ)]
    def f_hclr(e):
        e.memset(tailbuf[:], 0.0)
        return e.memset(xcr[:], 0.0)
    st_haloclr = P.step("vector", f_hclr,
        writes=["xcr", "tailbuf", "xcrhalo"] + XCRS, tag="haloclr")

    def stats_ssq(srcname, src_view, psn, tag, ones=None):
        rd = [srcname] if isinstance(srcname, str) else list(srcname)
        onesap = ones_c if ones is None else ones
        for r in range(ND):
            def f(e, r=r, onesap=onesap):
                return e.matmul(ps[psn][0:1, :], onesap[:],
                                sl(src_view, r), start=(r == 0),
                                stop=(r == ND - 1))
            P.step("tensor", f, reads=rd + ["consts"], writes=[psn],
                   tag=f"{tag}{r}")

    def rms_inv(psn, epsap, tag):
        P.step("scalar", lambda e: e.activation(
            out=stati[:], in_=ps[psn][0:1, :], func=AF.Sqrt,
            scale=1.0 / D, bias=epsap), reads=[psn, "consts"],
            writes=["stati"], tag=tag + "s")
        P.step("vector", lambda e: e.reciprocal(
            out=stati[:], in_=stati[:]), reads=["stati"],
            writes=["stati"], tag=tag + "r")

    def bcast(rowname, row_ap, psn, tag):
        P.step("tensor", lambda e: e.matmul(
            ps[psn][:], ones_rf[:], row_ap, start=True, stop=True),
            reads=[rowname, "consts"], writes=[psn], tag=tag)

    wiD = [hpool[:, 4096:4608], hpool[:, 4608:5120]]
    gate_patches = []
    ccdma_steps = {}
    init_steps = {}
    NG = NL * NCH

    for ch in range(NCH):
        cr = f"S{ch}"
        t0 = ch * TC
        def f_ldx(e, t0=t0):
            return e.dma_start(
                out=xorig[:, 0:ND * TC].rearrange("p (r t) -> p r t", t=TC),
                in_=xin.rearrange("(r p) t -> p r t", p=128)
                [:, :, t0:t0 + TC]).then_inc(CURSEM[0], 16)
        P.step("sync", f_ldx, writes=["crep"], inc=16, is_async=True,
               tag=f"{cr}_ldx")
        P.step("scalar", lambda e: e.activation(
            out=xsq[:, 0:ND * TC], in_=xorig[:, 0:ND * TC], func=AF.Square),
            reads=["crep"], writes=XCRS + ["xcrhalo"], tag=f"{cr}_sq")
        stats_ssq("crep", xorig, "psE", f"{cr}_sx")
        stats_ssq(XCRS, xsq, "psF", f"{cr}_sxx")
        def f_rows(e):
            e.tensor_scalar_mul(out=statm[:], in0=ps["psE"][0:1, :],
                                scalar1=1.0 / D)
            e.tensor_mul(out=stati[:], in0=statm[:], in1=statm[:])
            return e.scalar_tensor_tensor(
                out=stati[:], in0=ps["psF"][0:1, :], scalar=1.0 / D,
                in1=stati[:], op0=OP.mult, op1=OP.subtract)
        P.step("vector", f_rows, reads=["psE", "psF"],
               writes=["statm", "stati"], tag=f"{cr}_rows")
        P.step("scalar", lambda e: e.activation(
            out=stati[:], in_=stati[:], func=AF.Sqrt, bias=epsln[:]),
            reads=["stati", "consts"], writes=["stati"], tag=f"{cr}_invs")
        P.step("vector", lambda e: e.reciprocal(
            out=stati[:], in_=stati[:]), reads=["stati"],
            writes=["stati"], tag=f"{cr}_invr")
        bcast("statm", statm[:], "psG", f"{cr}_bm")
        bcast("stati", stati[:], "psH", f"{cr}_bi")
        for r in range(ND):
            def f_ap(e, r=r):
                e.tensor_sub(out=sl(xsq, r), in0=sl(xorig, r),
                             in1=ps["psG"][:])
                e.tensor_mul(out=sl(xsq, r), in0=sl(xsq, r),
                             in1=ps["psH"][:])
                return e.tensor_scalar(
                    out=sl(xcur, r), in0=sl(xsq, r),
                    scalar1=dco(DC_ONEPCA + r), scalar2=dco(DC_MOD + r),
                    op0=OP.mult, op1=OP.add)
            P.step("vector", f_ap,
                   reads=["crep", "psG", "psH", "dcol"],
                   writes=["xcur"] + XCRS, tag=f"{cr}_ap{r}")
        def f_st0(e, t0=t0):
            return e.dma_start(
                out=resid.rearrange("(r p) t -> p r t", p=128)
                [:, :, t0:t0 + TC],
                in_=xcur[:].rearrange("p (r t) -> p r t", t=TC)) \
                .then_inc(CURSEM[0], 16)
        P.step("sync", f_st0, reads=["xcur"], writes=["resid"],
               inc=16, is_async=True, tag=f"{cr}_st0")

    def emit_ldrf32(l, ch):
        t0 = ch * TC
        def f_ldr(e, t0=t0):
            return e.dma_start(
                out=xcur[:].rearrange("p (r t) -> p r t", t=TC),
                in_=resid.rearrange("(r p) t -> p r t", p=128)
                [:, :, t0:t0 + TC]).then_inc(CURSEM[0], 16)
        P.step("sync", f_ldr, reads=["resid"], writes=["xcur"],
               inc=16, is_async=True, tag=f"L{l}c{ch}_ldr")

    def emit_abgrp(l, ch, j):
        cr = f"L{l}c{ch}"
        for n in range(NST):
            ab = abar[n % 2]
            def f_ab(e, j=j, n=n, l=l, ab=ab):
                return e.activation(
                    out=ab[:], in_=sl(dlt, j), func=AF.Exp,
                    scale=apos[:, l * 256 + j * 16 + n:
                               l * 256 + j * 16 + n + 1])
            P.step("scalar", f_ab, reads=[f"dlt{j}", "apos"],
                   writes=[f"abar{n % 2}"], tag=f"{cr}_ab{j}_{n}")

    def emit_B(l, ch):
        cr = f"L{l}c{ch}"
        t0 = ch * TC
        def f_ldb(e, t0=t0):
            return e.dma_start(
                out=xf32.rearrange("p (r t) -> p r t", t=TC),
                in_=resid.rearrange("(r p) t -> p r t", p=128)
                [:, :, t0:t0 + TC]).then_inc(CURSEM[0], 16)
        P.step("sync", f_ldb, reads=["resid"], writes=XCRS + ["xcrhalo"],
               inc=16, is_async=True, tag=f"{cr}_ldb")
        P.step("scalar", lambda e: e.activation(
            out=xhat[:, 0:ND * TC], in_=xf32[:, 0:ND * TC], func=AF.Square),
            reads=XCRS, writes=["hpool"], tag=f"{cr}_rsq")
        stats_ssq("hpool", xhat, "psF", f"{cr}_rss", ones=ones_cb)
        rms_inv("psF", epsrms[:], f"{cr}_rinv")
        bcast("stati", stati[:], "psH", f"{cr}_rbi")
        for r in range(ND):
            def f_rap(e, r=r, l=l):
                return e.scalar_tensor_tensor(
                    out=sl(xhat, r), in0=sl(xf32, r),
                    scalar=ccol(OFF_NORMW + l * 8 + r), in1=ps["psH"][:],
                    op0=OP.mult, op1=OP.mult)
            P.step("vector", f_rap, reads=XCRS + ["psH", "cols"],
                   writes=["hpool"], tag=f"{cr}_rap{r}")

    def emit_C(l, ch):
        cr = f"L{l}c{ch}"
        for g in range(4):
            for k in range(8):
                def f_wld(e, l=l, k=k, g=g):
                    return e.dma_start(
                        out=wi[k][:], in_=wSin[l * 8 + k, :,
                                               g * 512:(g + 1) * 512]) \
                        .then_inc(CURSEM[0], 16)
                P.step("sync", f_wld, writes=[f"wi{k}"], inc=16,
                       is_async=True, deps=[win_rdy(l, k)],
                       tag=f"{cr}_wi{g}{k}")
            for mm in range(4):
                m = g * 4 + mm
                pA = PSN[m % 4]
                for k in range(8):
                    def f_mm(e, m=m, mm=mm, k=k, pA=pA):
                        return e.matmul(
                            ps[pA][:], wi[k][:, mm * 128:(mm + 1) * 128],
                            sl(xhat, k), start=(k == 0), stop=(k == 7))
                    P.step("tensor", f_mm,
                           reads=[f"wi{k}", "hpool"], writes=[pA],
                           tag=f"{cr}_ip{m}_{k}")
                def f_ev(e, m=m, pA=pA):
                    return e.copy(
                        out=xcr[:, m * XCW + 3:m * XCW + 3 + TC],
                        in_=ps[pA][:])
                P.step("scalar", f_ev, reads=[pA],
                       writes=[f"xcr{m}"], tag=f"{cr}_evx{m}")

    def emit_zgrp(l, ch, K):
        cr = f"L{l}c{ch}"
        g = 4 + K
        for k in range(8):
            def f_wld(e, l=l, k=k, g=g):
                return e.dma_start(
                    out=wi[k][:], in_=wSin[l * 8 + k, :,
                                           g * 512:(g + 1) * 512]) \
                    .then_inc(CURSEM[0], 16)
            P.step("sync", f_wld, writes=[f"wi{k}"], inc=16,
                   is_async=True, deps=[win_rdy(l, k)],
                   tag=f"{cr}_wi{g}{k}")
        for mm in range(4):
            m = g * 4 + mm
            pA = PSN[m % 4]
            for k in range(8):
                def f_mm(e, m=m, mm=mm, k=k, pA=pA):
                    return e.matmul(
                        ps[pA][:], wi[k][:, mm * 128:(mm + 1) * 128],
                        sl(xhat, k), start=(k == 0), stop=(k == 7))
                P.step("tensor", f_mm,
                       reads=[f"wi{k}", "hpool"], writes=[pA],
                       tag=f"{cr}_ip{m}_{k}")
            def f_ev(e, m=m, pA=pA):
                return e.activation(
                    out=sl(szb, m - NJ), in_=ps[pA][:],
                    func=AF.Silu)
            P.step("scalar", f_ev, reads=[pA],
                   writes=[f"szb{m - NJ}"], tag=f"{cr}_evz{m}")

    def emit_happ(l, ch):
        cr = f"L{l}c{ch}"
        def f_happ(e):
            return e.tensor_copy(
                out=xcr[:].rearrange("p (j w) -> p j w", w=XCW)[:, :, 0:3],
                in_=tailbuf[:].rearrange("p (j w) -> p j w", w=3))
        P.step("vector", f_happ, reads=["tailbuf"], writes=["xcrhalo"],
               tag=f"{cr}_happ")

    def emit_conv_slice(l, ch, j):
        cr = f"L{l}c{ch}"
        wt = wiD[j % 2]
        wtn = f"wiD{j % 2}"
        def f_wdl(e, l=l, j=j, wt=wt):
            return e.dma_start(out=wt, in_=wcd_d[l * 16 + j]) \
                .then_inc(CURSEM[0], 16)
        P.step("sync", f_wdl, writes=[wtn], inc=16, is_async=True,
               deps=prep_hp_guard, tag=f"{cr}_wdl{j}")
        pc = "psC" if j % 2 == 0 else "psD"
        for k in range(4):
            def f_cmm(e, j=j, k=k, pc=pc, wt=wt):
                base = j * XCW
                return e.matmul(ps[pc][:], wt[:, k * 128:(k + 1) * 128],
                                xcr[:, base + k:base + k + TC],
                                start=(k == 0), stop=(k == 3))
            P.step("tensor", f_cmm, reads=[wtn, f"xcr{j}", "xcrhalo"],
                   writes=[pc], tag=f"{cr}_cv{j}_{k}")
        def f_cs(e, j=j, l=l, pc=pc):
            return e.activation(
                out=sl(xcs, j), in_=ps[pc][:], func=AF.Silu,
                bias=ccol(OFF_CONVB + l * 16 + j))
        P.step("scalar", f_cs, reads=[pc, "cols"], writes=[f"xcs{j}"],
               tag=f"{cr}_cs{j}")
        def f_xpl(e, l=l, k=j):
            return e.dma_start(out=wxpt[k % 2][:],
                               in_=wSxp[l, :, k * 96:(k + 1) * 96]) \
                .then_inc(CURSEM[0], 16)
        P.step("sync", f_xpl, writes=[f"wxpt{j % 2}"], inc=16,
               is_async=True, deps=[wxp_rdy(l)], tag=f"{cr}_xpl{j}")
        def f_xp(e, k=j):
            return e.matmul(ps["psE"][0:96, :], wxpt[k % 2][:],
                            sl(xcs, k), start=(k == 0),
                            stop=(k == NJ - 1))
        P.step("tensor", f_xp, reads=[f"wxpt{j % 2}", f"xcs{j}"],
               writes=["psE"], tag=f"{cr}_xp{j}")

    def emit_xpe_bcst(l, ch):
        cr = f"L{l}c{ch}"
        P.step("scalar", lambda e: e.copy(out=xpb[:], in_=ps["psE"][0:96, :]),
               reads=["psE"], writes=["xpb"], tag=f"{cr}_xpe")
        P.step("sync", lambda e: e.dma_start(out=bcscr, in_=xpb[64:96, :])
               .then_inc(CURSEM[0], 16), reads=["xpb"],
               writes=["bcscr"], inc=16, is_async=True, tag=f"{cr}_bcst")

    def emit_bcs(l, ch):
        cr = f"L{l}c{ch}"
        def f_bc_s(e):
            return e.dma_start(
                out=brep[:].rearrange("p (n t) -> p n t", t=TC),
                in_=bcscr[0:NST, :].unsqueeze(0)
                .broadcast_to([128, NST, TC])).then_inc(CURSEM[0], 16)
        P.step("sync", f_bc_s, reads=["bcscr"], writes=["brep"],
               inc=16, is_async=True, tag=f"{cr}_bcs")

    def emit_bcg(l, ch):
        cr = f"L{l}c{ch}"
        def f_bc_g(e):
            return e.dma_start(
                out=crep[:].rearrange("p (n t) -> p n t", t=TC),
                in_=bcscr[NST:2 * NST, :].unsqueeze(0)
                .broadcast_to([128, NST, TC])).then_inc(CURSEM[0], 16)
        P.step("sync", f_bc_g, reads=["bcscr"], writes=["crep"],
               inc=16, is_async=True, tag=f"{cr}_bcg")

    def emit_dth(l, ch, half):
        cr = f"L{l}c{ch}"
        for q8 in range(8):
            mt = half * 8 + q8
            pA = PSN[q8]
            def f_dtl(e, l=l, mt=mt):
                return e.dma_start(
                    out=wdtt[mt % 2][:],
                    in_=wSdt[l, :, mt * 128:(mt + 1) * 128]) \
                    .then_inc(CURSEM[0], 16)
            P.step("sync", f_dtl, writes=[f"wdtt{mt % 2}"], inc=16,
                   is_async=True, deps=[wdt_rdy(l)],
                   tag=f"{cr}_dtl{mt}")
            def f_dt(e, mt=mt, pA=pA):
                return e.matmul(ps[pA][:], wdtt[mt % 2][:],
                                xpb[0:64, :], start=True, stop=True)
            P.step("tensor", f_dt, reads=[f"wdtt{mt % 2}", "xpb"],
                   writes=[pA], tag=f"{cr}_dt{mt}")
        for q8 in range(8):
            mt = half * 8 + q8
            pA = PSN[q8]
            def f_sg(e, mt=mt, l=l, pA=pA):
                return e.activation(
                    out=ps[pA][:], in_=ps[pA][:], func=AF.Sigmoid,
                    scale=-1.0, bias=dco(DC_NEGDTB + l * 16 + mt))
            P.step("scalar", f_sg, reads=[pA, "dcol"], writes=[pA],
                   tag=f"{cr}_sg{mt}")
        for q8 in range(8):
            mt = half * 8 + q8
            pA = PSN[q8]
            def f_ln(e, mt=mt, pA=pA):
                return e.activation(
                    out=sl(dlt, mt), in_=ps[pA][:], func=AF.Ln,
                    bias=eps37[:])
            P.step("scalar", f_ln, reads=[pA, "consts"],
                   writes=[f"dlt{mt}"], tag=f"{cr}_ln{mt}")

    def emit_Ftail(l, ch):
        cr = f"L{l}c{ch}"
        emit_xpe_bcst(l, ch)
        emit_bcs(l, ch)
        emit_dth(l, ch, 0)
        emit_dth(l, ch, 1)
        emit_bcg(l, ch)
        def f_tail(e):
            return e.tensor_copy(
                out=tailbuf[:].rearrange("p (j w) -> p j w", w=3),
                in_=xcr[:].rearrange("p (j w) -> p j w", w=XCW)
                [:, :, TC:TC + 3])
        P.step("vector", f_tail, reads=XCRS, writes=["tailbuf"],
               tag=f"{cr}_tail")
        for mt in range(NJ):
            def f_u(e, mt=mt):
                return e.tensor_mul(out=sl(u_s, mt), in0=sl(dlt, mt),
                                    in1=sl(xcs, mt))
            P.step("vector", f_u, reads=[f"dlt{mt}", f"xcs{mt}"],
                   writes=[f"u{mt}"], tag=f"{cr}_u{mt}")

    def emit_F0(l, ch):
        emit_happ(l, ch)
        for j in range(NJ):
            emit_conv_slice(l, ch, j)
        for K in range(4):
            emit_zgrp(l, ch, K)
        emit_Ftail(l, ch)

    def emit_Dj(l, ch, j):
        cr = f"L{l}c{ch}"
        def f_bxp(e, j=j):
            return e.tensor_mul(
                out=hcpool[:].rearrange("p (n t) -> p n t", t=TC),
                in0=sl(u_s, j).unsqueeze(1)
                .broadcast_to([128, NST, TC]),
                in1=brep[:].rearrange("p (n t) -> p n t", t=TC))
        P.step("vector", f_bxp, reads=[f"u{j}", "brep"],
               writes=["hcpool"], tag=f"{cr}_bxp{j}")
        for n in range(NST):
            ab = abar[n % 2]
            def f_sc(e, j=j, n=n, ab=ab):
                return e.tensor_tensor_scan(
                    out=sl(hcpool, n), data0=ab[:], data1=sl(hcpool, n),
                    initial=state[:, j * 16 + n:j * 16 + n + 1],
                    op0=OP.mult, op1=OP.add)
            P.step("vector", f_sc,
                   reads=[f"abar{n % 2}", "hcpool", "state"],
                   writes=["hcpool"], tag=f"{cr}_sc{j}_{n}")
        def f_st(e, j=j):
            return e.tensor_copy(
                out=state[:, j * 16:(j + 1) * 16],
                in_=hcpool[:].rearrange("p (n t) -> p n t", t=TC)
                [:, :, TC - 1])
        P.step("vector", f_st, reads=["hcpool"], writes=["state"],
               tag=f"{cr}_stc{j}")
        def f_hcp(e, j=j):
            return e.tensor_mul(out=hcpool[:], in0=hcpool[:],
                                in1=crep[:])
        P.step("vector", f_hcp, reads=["hcpool", "crep"],
               writes=["hcpool"], tag=f"{cr}_hcp{j}")
        def f_tr(e, j=j):
            for st_ in (1, 2, 4):
                v = hcpool[:].rearrange("p (a g t) -> p a g t",
                                        g=2 * st_, t=TC)
                e.tensor_add(out=v[:, :, 0, :],
                             in0=v[:, :, 0, :],
                             in1=v[:, :, st_, :])
            return e.tensor_add(out=sl(ytot, j), in0=sl(hcpool, 0),
                                in1=sl(hcpool, 8))
        P.step("vector", f_tr, reads=["hcpool"],
               writes=[f"ytot{j}", "hcpool"], tag=f"{cr}_tr{j}")
        def f_yd(e, j=j, l=l):
            return e.scalar_tensor_tensor(
                out=sl(ytot, j), in0=sl(xcs, j),
                scalar=ccol(OFF_DP + l * 16 + j), in1=sl(ytot, j),
                op0=OP.mult, op1=OP.subtract)
        P.step("vector", f_yd,
               reads=[f"xcs{j}", "cols", f"ytot{j}"],
               writes=[f"ytot{j}"], tag=f"{cr}_yd{j}")
        def f_yg(e, j=j):
            return e.tensor_mul(out=sl(ytot, j), in0=sl(ytot, j),
                                in1=sl(szb, j))
        st_yg = P.step("vector", f_yg,
                       reads=[f"ytot{j}", f"szb{j}"], writes=[f"ytot{j}"],
                       tag=f"{cr}_yg{j}")
        if j + 1 < NJ:
            emit_abgrp(l, ch, j + 1)
        return st_yg

    def emit_E_mm(l, ch):
        cr = f"L{l}c{ch}"
        for j in range(NJ):
            def f_wo(e, l=l, j=j):
                return e.dma_start(out=wo[j % 2][:],
                                   in_=wSout[l * 16 + j]) \
                    .then_inc(CURSEM[0], 16)
            P.step("sync", f_wo, writes=[f"wo{j % 2}"], inc=16,
                   is_async=True, deps=[wout_rdy(l, j)],
                   tag=f"{cr}_wo{j}")
            for m in range(8):
                def f_op(e, k=j, m=m):
                    return e.matmul(
                        ps[PSN[m]][:], wo[k % 2][:, m * 128:(m + 1) * 128],
                        sl(ytot, k), start=(k == 0), stop=(k == NJ - 1))
                P.step("tensor", f_op, reads=[f"wo{j % 2}", f"ytot{j}"],
                       writes=[PSN[m]], tag=f"{cr}_op{j}_{m}")

    def emit_E_fin(l, ch):
        cr = f"L{l}c{ch}"
        t0 = ch * TC
        for m in range(8):
            def f_ra(e, m=m):
                return e.tensor_add(out=sl(xcur, m), in0=ps[PSN[m]][:],
                                    in1=sl(xcur, m))
            P.step("vector", f_ra, reads=[PSN[m], "xcur"],
                   writes=["xcur"], tag=f"{cr}_ra{m}")
        def f_str(e, t0=t0):
            return e.dma_start(
                out=resid.rearrange("(r p) t -> p r t", p=128)
                [:, :, t0:t0 + TC],
                in_=xcur[:].rearrange("p (r t) -> p r t", t=TC)) \
                .then_inc(CURSEM[0], 16)
        P.step("sync", f_str, reads=["xcur"], writes=["resid"],
               inc=16, is_async=True, tag=f"{cr}_str")

    emit_ldrf32(0, 0)
    emit_B(0, 0)
    emit_C(0, 0)

    last_yg = None
    for l in range(NL):
        lr = f"L{l}"
        prevcc = ccdma_steps.get(l - 1, st_ccinit)
        st1 = P.step("vector", (lambda l=l: lambda e: e.tensor_scalar_mul(
            out=state[:], in0=ccres[:, 0:256], scalar1=ccol(OFF_SEL)))(),
            reads=["cols", "flags"], writes=["state"],
            deps=[prevcc], tag=f"{lr}_stinit")
        gate_patches.append((st1, l))
        st2 = P.step("vector", (lambda l=l: lambda e: e.tensor_scalar_mul(
            out=tailbuf[:], in0=ccres[:, 256:304], scalar1=ccol(OFF_SEL)))(),
            reads=["cols", "flags"], writes=["tailbuf"],
            deps=[prevcc], tag=f"{lr}_hinit")
        gate_patches.append((st2, l))
        init_steps[l] = (st1, st2)
        emit_F0(l, 0)

        for ch in range(NCH):
            g = l * NCH + ch
            sliced = False
            if g + 1 < NG:
                l2, c2 = (g + 1) // NCH, (g + 1) % NCH
                emit_B(l2, c2)
            emit_abgrp(l, ch, 0)
            if g + 1 < NG:
                emit_C(l2, c2)
                if c2 != 0:
                    sliced = True
                    emit_happ(l2, c2)
            if g >= 1:
                l1, c1 = (g - 1) // NCH, (g - 1) % NCH
                emit_E_mm(l1, c1)
            for jb in range(NJ):
                last_yg = emit_Dj(l, ch, jb)
                if jb == 2 and g >= 1:
                    emit_E_fin(l1, c1)
                    emit_ldrf32(l, ch)
                if sliced:
                    if jb == 3:
                        for j2 in range(4):
                            emit_conv_slice(l2, c2, j2)
                    elif jb >= 4:
                        emit_conv_slice(l2, c2, jb)
                    if jb in (3, 7, 11):
                        emit_zgrp(l2, c2, (jb - 3) // 4)
            if sliced:
                emit_zgrp(l2, c2, 3)
                emit_Ftail(l2, c2)

        stg1 = P.step("gpsimd", lambda e: e.dma_start(
            out=cc_in[:, 0:256], in_=state[:]).then_inc(CURSEM[0], 16),
            reads=[], writes=["cc_in"], no_raw=True, inc=16, is_async=True,
            deps=[P.writer.get("cc_out"), st_ccinit, st_haloclr],
            tag=f"{lr}_stg1")
        stg1.reg_wait = ("vector", last_yg, 1)
        stg2 = P.step("gpsimd", lambda e: e.dma_start(
            out=cc_in[:, 256:304].rearrange("p (j w) -> p j w", w=3),
            in_=tailbuf[:].rearrange("p (j w) -> p j w", w=3))
            .then_inc(CURSEM[0], 16),
            reads=[], writes=["cc_in"], no_raw=True, inc=16, is_async=True,
            deps=[st_ccinit, st_haloclr], tag=f"{lr}_stg2")
        stg2.reg_wait = ("vector", last_yg, 1)
        P.readers.setdefault("state", []).append(stg1.idx)
        P.readers.setdefault("tailbuf", []).append(stg2.idx)
        ccstep = P.step("gpsimd", lambda e: e.collective_compute(
            "AllGather", OP.bypass,
            replica_groups=groups or [[0, 1], [2, 3], [4, 5], [6, 7]],
            ins=[cc_in], outs=[cc_out]).then_inc(sems["gpsimd"], 1),
            reads=["cc_in"], writes=["cc_out"], inc=1, is_async=True,
            tag=f"{lr}_cc")
        ccd = P.step("gpsimd", (lambda l=l: lambda e: e.dma_start(
            out=ccres[:], in_=cc_out[0]).then_inc(CURSEM[0], 16))(),
            reads=["cc_out"], inc=16, is_async=True,
            deps=(list(init_steps[l - 1]) if (l - 1) in init_steps else []),
            tag=f"{lr}_ccd")
        ccdma_steps[l] = ccd

    emit_E_mm(NL - 1, NCH - 1)
    emit_E_fin(NL - 1, NCH - 1)

    for st, l in gate_patches:
        if l in ccdma_steps:
            st.reg_wait = ("gpsimd", ccdma_steps[l], 0)

    out_dmas = []
    if do_ffn:
        for ch in range(NCH):
            cr = f"Fc{ch}"
            t0 = ch * TC
            def f_ldr(e, t0=t0):
                return e.dma_start(
                    out=xcur[:].rearrange("p (r t) -> p r t", t=TC),
                    in_=resid.rearrange("(r p) t -> p r t", p=128)
                    [:, :, t0:t0 + TC]).then_inc(CURSEM[0], 16)
            P.step("sync", f_ldr, reads=["resid"], writes=["xcur"], inc=16,
                   is_async=True, tag=f"{cr}_ldr")
            def f_ldx(e, t0=t0):
                return e.dma_start(
                    out=xorig[:, 0:ND * TC].rearrange("p (r t) -> p r t", t=TC),
                    in_=xin.rearrange("(r p) t -> p r t", p=128)
                    [:, :, t0:t0 + TC]).then_inc(CURSEM[0], 16)
            P.step("sync", f_ldx, writes=["crep"], inc=16, is_async=True,
                   tag=f"{cr}_ldx")
            def f_ldm(e, t0=t0):
                return e.dma_start(
                    out=maskv[:, 0:ND * TC].rearrange("p (r t) -> p r t", t=TC),
                    in_=maskin.rearrange("(r p) t -> p r t", p=128)
                    [:, :, t0:t0 + TC]).then_inc(CURSEM[0], 16)
            P.step("sync", f_ldm, writes=["brep"], inc=16, is_async=True,
                   tag=f"{cr}_ldm")

            P.step("scalar", lambda e: e.activation(
                out=xsq[:, 0:ND * TC], in_=xcur[:, 0:ND * TC], func=AF.Square),
                reads=["xcur"], writes=["xcr"], tag=f"{cr}_fsq")
            stats_ssq("xcr", xsq, "psF", f"{cr}_fss")
            rms_inv("psF", epsrms[:], f"{cr}_finv")
            bcast("stati", stati[:], "psH", f"{cr}_fbi")
            for r in range(ND):
                def f_so(e, r=r):
                    return e.scalar_tensor_tensor(
                        out=sl(xsq, r), in0=sl(xcur, r),
                        scalar=ccol(OFF_FINW + r), in1=ps["psH"][:],
                        op0=OP.mult, op1=OP.mult)
                P.step("vector", f_so, reads=["xcur", "psH", "cols"],
                       writes=["xcr"], tag=f"{cr}_so{r}")
            for r in range(ND):
                def f_xo(e, r=r):
                    return e.scalar_tensor_tensor(
                        out=sl(xcur, r), in0=sl(xsq, r),
                        scalar=dco(DC_GA + r), in1=sl(xorig, r),
                        op0=OP.mult, op1=OP.add)
                P.step("vector", f_xo, reads=["xcr", "crep", "dcol"],
                       writes=["xcur"], tag=f"{cr}_xo{r}")

            P.step("scalar", lambda e: e.activation(
                out=xsq[:, 0:ND * TC], in_=xcur[:, 0:ND * TC], func=AF.Square),
                reads=["xcur"], writes=["xcr"], tag=f"{cr}_2sq")
            stats_ssq("xcur", xcur, "psE", f"{cr}_2sx")
            stats_ssq("xcr", xsq, "psF", f"{cr}_2sxx")
            def f_rows2(e):
                e.tensor_scalar_mul(out=statm[:], in0=ps["psE"][0:1, :],
                                    scalar1=1.0 / D)
                e.tensor_mul(out=statv, in0=statm[:], in1=statm[:])
                return e.scalar_tensor_tensor(
                    out=statv, in0=ps["psF"][0:1, :],
                    scalar=1.0 / D, in1=statv,
                    op0=OP.mult, op1=OP.subtract)
            P.step("vector", f_rows2, reads=["psE", "psF"],
                   writes=["statm", "ytot"], tag=f"{cr}_2rows")
            P.step("scalar", lambda e: e.activation(
                out=statv, in_=statv, func=AF.Sqrt, bias=epsln[:]),
                reads=["ytot", "consts"], writes=["ytot"], tag=f"{cr}_2invs")
            P.step("vector", lambda e: e.reciprocal(
                out=stati[:], in_=statv), reads=["ytot"],
                writes=["stati"], tag=f"{cr}_2invr")
            bcast("statm", statm[:], "psG", f"{cr}_2bm")
            bcast("stati", stati[:], "psH", f"{cr}_2bi")
            for r in range(ND):
                def f_q(e, r=r):
                    e.tensor_sub(out=sl(xsq, r), in0=sl(xcur, r),
                                 in1=ps["psG"][:])
                    e.tensor_mul(out=sl(xsq, r), in0=sl(xsq, r),
                                 in1=ps["psH"][:])
                    return e.tensor_scalar(
                        out=sl(xqv, r), in0=sl(xsq, r),
                        scalar1=dco(DC_ONEPCB + r), scalar2=dco(DC_MOD + 24 + r),
                        op0=OP.mult, op1=OP.add)
                P.step("vector", f_q, reads=["xcur", "psG", "psH", "dcol"],
                       writes=["szb", "xcr"], tag=f"{cr}_q{r}")

            for k in range(8):
                vw = wf8a if k < 4 else wf8b
                def f_w1(e, k=k, vw=vw):
                    return e.dma_start(
                        out=vw[:, (k % 4) * 4096:(k % 4) * 4096 + 4096],
                        in_=wS1[k]).then_inc(CURSEM[0], 16)
                P.step("sync", f_w1,
                       writes=[f"wf{k}", "xcs" if k < 4 else "xcr"],
                       inc=16, is_async=True,
                       deps=[w1_rdy(k)], tag=f"{cr}_w1l{k}")
            for m in range(32):
                pA = PSN[m % 8]
                for k in range(8):
                    vw = wf8a if k < 4 else wf8b
                    def f_m1(e, m=m, k=k, vw=vw, pA=pA):
                        return e.matmul(
                            ps[pA][:],
                            vw[:, (k % 4) * 4096 + m * 128:
                               (k % 4) * 4096 + (m + 1) * 128],
                            xqv[:, k * TC:(k + 1) * TC],
                            start=(k == 0), stop=(k == 7))
                    P.step("tensor", f_m1, reads=[f"wf{k}", "szb"],
                           writes=[pA], tag=f"{cr}_m1{m}_{k}")
                hf = hflo if m < 16 else hfhi
                def f_e1(e, m=m, pA=pA, hf=hf):
                    return e.activation(
                        out=hf[:, (m % 16) * TC:(m % 16 + 1) * TC],
                        in_=ps[pA][:], func=AF.Relu,
                        bias=dco(DC_B1Q + m))
                P.step("scalar", f_e1, reads=[pA, "dcol"],
                       writes=["hpool" if m < 16 else "hcpool"],
                       tag=f"{cr}_e1{m}")
            def f_c8(e):
                e.tensor_copy(out=h8v[:, 0:8192], in_=hflo[:, 0:8192])
                return e.tensor_copy(out=h8v[:, 8192:16384],
                                     in_=hfhi[:, 0:8192])
            P.step("vector", f_c8, reads=["hpool", "hcpool"], writes=["u_s"],
                   tag=f"{cr}_c8")

            for k in range(32):
                def f_w2(e, k=k):
                    return e.dma_start(out=wo8[k % 2][:, 0:1024],
                                       in_=wS2[k]).then_inc(CURSEM[0], 16)
                P.step("sync", f_w2, writes=[f"wo{k % 2}"], inc=16,
                       is_async=True, deps=[w2_rdy(k)], tag=f"{cr}_w2l{k}")
                for m in range(8):
                    def f_m2(e, k=k, m=m):
                        return e.matmul(
                            ps[PSN[m]][:],
                            wo8[k % 2][:, m * 128:(m + 1) * 128],
                            h8v[:, k * TC:(k + 1) * TC],
                            start=(k == 0), stop=(k == 31))
                    P.step("tensor", f_m2, reads=[f"wo{k % 2}", "u_s"],
                           writes=[PSN[m]], tag=f"{cr}_m2{k}_{m}")
            for m in range(8):
                def f_e2(e, m=m):
                    return e.activation(
                        out=y2v[:, m * TC:(m + 1) * TC], in_=ps[PSN[m]][:],
                        func=AF.Identity, bias=dco(DC_B2Q + m))
                P.step("scalar", f_e2, reads=[PSN[m], "dcol"], writes=["dlt"],
                       tag=f"{cr}_e2{m}")

            for r in range(ND):
                def f_mk(e, r=r):
                    return e.tensor_mul(out=sl(xsq, r),
                                        in0=y2v[:, r * TC:(r + 1) * TC],
                                        in1=maskv[:, r * TC:(r + 1) * TC])
                P.step("vector", f_mk, reads=["dlt", "brep"],
                       writes=["xcr"], tag=f"{cr}_mk{r}")
                def f_fin(e, r=r):
                    return e.scalar_tensor_tensor(
                        out=sl(xcur, r), in0=sl(xsq, r),
                        scalar=dco(DC_GB + r), in1=sl(xcur, r),
                        op0=OP.mult, op1=OP.add)
                P.step("vector", f_fin, reads=["xcr", "dcol", "xcur"],
                       writes=["xcur"], tag=f"{cr}_fin{r}")
            st_out = P.step("gpsimd", (lambda t0=t0: lambda e: e.dma_start(
                out=xout.rearrange("(r p) t -> p r t", p=128)[:, :, t0:t0 + TC],
                in_=xcur[:].rearrange("p (r t) -> p r t", t=TC))
                .then_inc(CURSEM[0], 16))(),
                reads=["xcur"], writes=["xout"], inc=16, is_async=True,
                tag=f"{cr}_out")
            out_dmas.append(st_out)
    else:
        for ch in range(NCH):
            t0 = ch * TC
            def f_cp(e, t0=t0):
                return e.dma_start(
                    out=xout.rearrange("(r p) t -> p r t", p=128)
                    [:, :, t0:t0 + TC],
                    in_=resid.rearrange("(r p) t -> p r t", p=128)
                    [:, :, t0:t0 + TC]).then_inc(CURSEM[0], 16)
            st_out = P.step("gpsimd", f_cp, reads=["resid"], writes=["xout"],
                            inc=16, is_async=True, tag=f"cp{ch}")
            out_dmas.append(st_out)

    if dbg is not None:
        dmap = {
            "xq": (szb[:].bitcast(F8)[:, 0:4096], "szb"),
            "h8": (u_s[:].bitcast(F8)[:, 0:16384], "u_s"),
            "y2": (dlt[:].bitcast(F16)[:, 0:4096], "dlt"),
            "xcur": (xcur[:], "xcur"),
            "slotout": (hpool[:].bitcast(F32)[:, 0:4096], "hpool"),
            "ytot": (ytot[:], "ytot"),
            "szb": (szb[:], "szb"),
            "dlt": (dlt[:], "dlt"),
            "xcs": (xcs[:], "xcs"),
            "dcol": (dcol[:], "dcol"),
            "apos": (apos[:], "apos"),
        }
        src_ap, bufname = dmap[dbg[0]]
        st_d = P.step("gpsimd", lambda e: e.dma_start(out=dbg_out, in_=src_ap)
                      .then_inc(CURSEM[0], 16), reads=[bufname],
                      writes=["dbgout"], inc=16, is_async=True, tag="dbg")
        out_dmas.append(st_d)

    fin = P.step("gpsimd", lambda e: e.engine_nop(),
                 deps=[s.idx for s in out_dmas], tag="final")

    P.finalize()
    if os.environ.get("KDUMP"):
        with open("/tmp/steps.txt", "w") as fh:
            for s in P.steps:
                fh.write(f"{s.idx} {s.eng} {s.sem} {s.sem_val} {s.tag} "
                         f"deps={sorted(s.deps)}\n")

    eng_steps = {e: [s for s in P.steps if s.eng == e] for e in Prog.ENGS}

    def emit(eng_name, eng):
        seen = {e: 0 for e in semnames}
        for s in eng_steps[eng_name]:
            for d in sorted(s.deps):
                dst = P.steps[d]
                if dst.eng == eng_name and not dst.is_async:
                    continue
                if dst.sem_val > seen[dst.sem]:
                    eng.wait_ge(sems[dst.sem], dst.sem_val)
                    seen[dst.sem] = dst.sem_val
            if s.reg_wait is not None:
                weng, wstep, fidx = s.reg_wait
                fl = eng.value_load(flags[0:1, fidx:fidx + 1])
                tgt = eng.compute_val(fl * int(wstep.sem_val))
                eng.wait_ge(sems[wstep.sem], tgt)
            if s.is_async and s.sem not in Prog.ENGS:
                prev = s.sem_val - s.inc
                if prev > seen[s.sem]:
                    eng.wait_ge(sems[s.sem], prev)
                    seen[s.sem] = prev
            CURSEM[0] = sems[s.sem]
            inst = s.fn(eng)
            if inst is not None and s.inc == 1 and not s.is_async:
                inst.then_inc(sems[s.sem], 1)

    with nc.Block() as block:
        @block.sync
        def _(e):
            emit("sync", e)

        @block.vector
        def _(e):
            emit("vector", e)

        @block.scalar
        def _(e):
            emit("scalar", e)

        @block.tensor
        def _(e):
            emit("tensor", e)

        @block.gpsimd
        def _(e):
            emit("gpsimd", e)

    es.close()
    nc.finalize()
    return nc


def _prep_inputs(inputs):
    import jax
    x = np.ascontiguousarray(np.asarray(inputs["L0_attn_x"], np.float32))
    c = np.asarray(inputs["L0_c"], np.float32)
    wmod = np.asarray(inputs["L0_wmod"], np.float32)
    norm_w = np.asarray(inputs["L0_attn_norm_w"], np.float32)
    in_w = np.asarray(inputs["L0_attn_in_w"], np.float32)
    conv_w = np.asarray(inputs["L0_attn_conv_w"], np.float32)
    conv_b = np.asarray(inputs["L0_attn_conv_b"], np.float32)
    A_log = np.asarray(inputs["L0_attn_A_log"], np.float32)
    Dp = np.asarray(inputs["L0_attn_Dp"], np.float32)
    xproj_w = np.asarray(inputs["L0_attn_xproj_w"], np.float32)
    dt_w = np.asarray(inputs["L0_attn_dt_w"], np.float32)
    dt_b = np.asarray(inputs["L0_attn_dt_b"], np.float32)
    out_w = np.asarray(inputs["L0_attn_out_w"], np.float32)
    final_w = np.asarray(inputs["L0_attn_final_w"], np.float32)
    w1 = np.asarray(inputs["L0_ffn_w1"], np.float32)
    b1 = np.asarray(inputs["L0_ffn_b1"], np.float32)
    w2 = np.asarray(inputs["L0_ffn_w2"], np.float32)
    b2 = np.asarray(inputs["L0_ffn_b2"], np.float32)

    with jax.default_device(jax.devices("cpu")[0]):
        with jax.default_prng_impl("threefry2x32"):
            keep = np.asarray(jax.random.bernoulli(
                jax.random.key(42), 0.8, (4, 4096, D)))
    maskf = (keep.astype(np.float32) * 1.25).astype(np.float16)

    w_in = np.ascontiguousarray(
        np.stack([in_w[l].T.reshape(8, 128, 4096) for l in range(NL)])
        .reshape(NL * 8, 128, 4096))
    w_out = np.ascontiguousarray(
        np.stack([out_w[l].T.reshape(16, 128, 1024) for l in range(NL)])
        .reshape(NL * 16, 128, 1024))
    w_xp = np.ascontiguousarray(
        np.stack([xproj_w[l].T.reshape(16, 128, 96)
                  .transpose(1, 0, 2).reshape(128, 16 * 96)
                  for l in range(NL)]))
    w_dt = np.ascontiguousarray(
        np.stack([dt_w[l].T for l in range(NL)]))
    w1T = np.ascontiguousarray(w1.T.reshape(8, 128, 4096))
    w2T = np.ascontiguousarray(w2.T.reshape(32, 128, 1024))
    wmod_s = np.ascontiguousarray(wmod.reshape(8, 128, 6144))

    idx = np.arange(128)
    wcd = np.zeros((NL, 16, 128, 4, 128), ml_dtypes.bfloat16)
    for l in range(NL):
        cw = conv_w[l][:, 0, :].reshape(16, 128, 4)
        for j in range(16):
            for k in range(4):
                wcd[l, j, idx, k, idx] = cw[j, :, k].astype(
                    ml_dtypes.bfloat16)
    wconvd = np.ascontiguousarray(wcd.reshape(NL * 16, 128, 512))

    alog = np.ascontiguousarray(np.concatenate(
        [A_log[l].reshape(16, 128, 16).transpose(1, 0, 2).reshape(128, 256)
         for l in range(NL)], axis=1))

    def colpack(b):
        colsx = np.zeros((128, NCOLS), np.float32)
        for l in range(NL):
            cw = conv_w[l][:, 0, :].reshape(16, 128, 4)
            colsx[:, OFF_CONVW + l * 64:OFF_CONVW + (l + 1) * 64] = \
                cw.transpose(1, 0, 2).reshape(128, 64)
            colsx[:, OFF_CONVB + l * 16:OFF_CONVB + (l + 1) * 16] = \
                conv_b[l].reshape(16, 128).T
            colsx[:, OFF_DP + l * 16:OFF_DP + (l + 1) * 16] = \
                Dp[l].reshape(16, 128).T
            colsx[:, OFF_DTB + l * 16:OFF_DTB + (l + 1) * 16] = \
                dt_b[l].reshape(16, 128).T
            colsx[:, OFF_NORMW + l * 8:OFF_NORMW + (l + 1) * 8] = \
                norm_w[l].reshape(8, 128).T
        colsx[:, OFF_FINW:OFF_FINW + 8] = final_w.reshape(8, 128).T
        colsx[:, OFF_C:OFF_C + 8] = c[b].reshape(8, 128).T
        colsx[:, OFF_B1:OFF_B1 + 32] = b1.reshape(32, 128).T
        colsx[:, OFF_B2:OFF_B2 + 8] = b2.reshape(8, 128).T
        return colsx

    in_maps = []
    for core in range(8):
        b, h = core >> 1, core & 1
        colsx = colpack(b).copy()
        colsx[:, OFF_SEL] = float(h)
        xf = np.ascontiguousarray(x[b].T[:, h * LC:(h + 1) * LC])
        mf = np.ascontiguousarray(maskf[b].T[:, h * LC:(h + 1) * LC])
        in_maps.append({
            "xin": xf, "maskin": mf, "cols": colsx, "alog": alog,
            "id8": np.eye(128, dtype=ml_dtypes.float8_e4m3),
            "flags": np.array([[h, 1 - h]], np.int32),
            "w_in": w_in, "w_out": w_out, "w_xp": w_xp, "w_dt": w_dt,
            "w1": w1T, "w2": w2T, "wmod": wmod_s, "wconvd": wconvd,
        })
    return in_maps


_NC_CACHE = {}


def kernel(**inputs):
    key = ("full",)
    if key not in _NC_CACHE:
        _NC_CACHE[key] = build()
    nc = _NC_CACHE[key]
    in_maps = _prep_inputs(inputs)
    res = run_bass_kernel_spmd(nc, in_maps, core_ids=list(range(8)))
    out = np.zeros((4, 4096, D), np.float32)
    for core in range(8):
        b, h = core >> 1, core & 1
        out[b, h * LC:(h + 1) * LC, :] = res.results[core]["xout"].T
    return out
